# version 05
# baseline (speedup 1.0000x reference)
import os

os.environ.setdefault("GAUGE_NTFF_IGNORE_DMA", "1")

import numpy as np
import ml_dtypes

import concourse.bass as bass
import concourse.mybir as mybir
from concourse.bass_utils import run_bass_kernel_spmd

F32 = mybir.dt.float32
BF16 = mybir.dt.bfloat16
FP8 = mybir.dt.float8e4
U8 = mybir.dt.uint8
AF = mybir.ActivationFunctionType
ALU = mybir.AluOpType

BIG = 1.0e30
EPS = 1.0e-5


def full_cfg():
    return dict(D=2048, H=16, T=2048, PRE=6144, HID=8192, FBLK=1024)


def make_consts(cfg):
    D, H, PRE = cfg["D"], cfg["H"], cfg["PRE"]
    DH = D // H
    NPRE = PRE // 128
    j = np.arange(128)[:, None].astype(np.float32)
    i = np.arange(128)[None, :].astype(np.float32)

    blk = i - j
    blk[i < j] = BIG
    m_d = np.tile(blk, (1, H)).astype(np.float32)

    m_pre = np.full((128, NPRE * 128), BIG, np.float32)
    for c in range(NPRE):
        e = (PRE - 1 - (c * 128 + np.arange(128))).astype(np.float32)
        m_pre[:, c * 128:c * 128 + H] = e[:, None]

    m_l3 = np.full((128, 128), BIG, np.float32)
    m_l3[:H, :] = (np.arange(128) + 1.0)[None, :]

    m_sum = np.full((128, 128), BIG, np.float32)
    m_sum[:, :H] = (127.0 - np.arange(128))[:, None]

    mask16 = np.zeros((128, D), np.float32)
    for h in range(H):
        mask16[h, h * DH:(h + 1) * DH] = 1.0

    ident = np.eye(128, dtype=np.float32)
    return dict(m_d=m_d, m_pre=m_pre, m_l3=m_l3, m_sum=m_sum,
                mask16=mask16.astype(ml_dtypes.bfloat16), ident=ident)


class _Dummy:
    __slots__ = ()
    def __getattr__(self, name):
        return _dummy_call
def _dummy_call(*a, **k):
    return _DUMMY
_DUMMY = _Dummy()

ENGINES = ("sync", "gpsimd", "tensor", "vector", "scalar")
COMPUTE_SEMS = ("sT", "sV", "sS")
DMA_SEMS = ("c0", "g0", "dbx", "wme", "wmo", "xe", "xo", "ys", "yl",
            "x2se", "x2so", "ose", "oso", "xpe", "xpo", "wse", "wso",
            "wge", "wgo", "cv1", "cv2", "cv3", "fge", "fgo", "fle", "flo")
SEMS = COMPUTE_SEMS + DMA_SEMS


class P:
    def __init__(self, nc, cur, sems, real_engine):
        self.nc = nc
        self.cur = cur
        self.sems = sems
        self.cnt = {k: 0 for k in sems}
        for e in ENGINES:
            setattr(self, e, real_engine if e == cur else _DUMMY)

    def inc(self, inst, sem, n=1):
        self.cnt[sem] += n
        if inst is not _DUMMY:
            inst.then_inc(self.sems[sem], n)
        return self.cnt[sem]

    def dinc(self, inst, sem):
        v = self.inc(inst, sem, 16)
        return (sem, v)

    def wait(self, eng, sem, val=None):
        v = self.cnt[sem] if val is None else val
        if v > 0:
            getattr(self, eng).wait_ge(self.sems[sem], v)

    def waitm(self, eng, mark):
        self.wait(eng, mark[0], mark[1])

    def at(self, sem):
        return self.cnt[sem]


def build_program(cfg, debug_taps=()):
    D, H, T, PRE, HID, FBLK = (cfg[k] for k in ("D", "H", "T", "PRE", "HID", "FBLK"))
    DC = D // 128
    NC = T // 128
    NPRE = PRE // 128
    KH = HID // 128
    NBLK = T // FBLK
    FTC = FBLK // 128
    NDB = D // 512
    NJ = 6 * D // 512
    NH2 = FBLK // 512
    NG = NDB * 2

    nc = bass.Bass("TRN2", target_bir_lowering=False, detect_race_conditions=True)

    dram = lambda n, s, dt, kind: nc.dram_tensor(n, s, dt, kind=kind).ap()
    x_in = dram("x", [T, D], F32, "ExternalInput")
    xpre_in = dram("xpre", [PRE, D], F32, "ExternalInput")
    wq_in = dram("wq", [D, D], F32, "ExternalInput")
    wv_in = dram("wv", [D, D], F32, "ExternalInput")
    wg_in = dram("wg", [D, D], F32, "ExternalInput")
    wo_in = dram("wo", [D, D], F32, "ExternalInput")
    wg0_in = dram("wg0", [D, HID], F32, "ExternalInput")
    wg1_in = dram("wg1", [D, HID], F32, "ExternalInput")
    wl_in = dram("wl", [HID, D], F32, "ExternalInput")
    wmod_in = dram("wmod", [D, 6 * D], F32, "ExternalInput")
    cvec_in = dram("cvec", [1, D], F32, "ExternalInput")
    beta_in = dram("beta", [1, H], F32, "ExternalInput")
    lnw_in = dram("lnw", [1, D], F32, "ExternalInput")
    lnb_in = dram("lnb", [1, D], F32, "ExternalInput")
    m_d_in = dram("m_d", [128, D], F32, "ExternalInput")
    m_pre_in = dram("m_pre", [128, PRE], F32, "ExternalInput")
    m_l3_in = dram("m_l3", [128, 128], F32, "ExternalInput")
    m_sum_in = dram("m_sum", [128, 128], F32, "ExternalInput")
    mask16_in = dram("mask16", [128, D], BF16, "ExternalInput")
    ident_in = dram("ident", [128, 128], F32, "ExternalInput")
    mreal_in = dram("mreal", [128, NPRE], BF16, "ExternalInput")

    out_t = dram("out", [T, D], F32, "ExternalOutput")

    yhat_s = dram("yhat_s", [T, D], BF16, "Internal")
    x2_s = dram("x2_s", [T, D], F32, "Internal")
    wg0f8 = dram("wg0f8", [D, HID], FP8, "Internal")
    wg1f8 = dram("wg1f8", [D, HID], FP8, "Internal")
    wlf8 = dram("wlf8", [HID, D], FP8, "Internal")
    wgbf = dram("wgbf", [D, D], BF16, "Internal")
    wobf = dram("wobf", [D, D], BF16, "Internal")

    dbg = {}
    def mkdbg(name, shape, dtype=F32):
        if name in debug_taps:
            dbg[name] = dram("dbg_" + name, shape, dtype, "ExternalOutput")
    mkdbg("lam", [1, H]); mkdbg("dall", [128, D], BF16); mkdbg("lpre", [128, PRE], BF16)
    mkdbg("modrow", [1, 6 * D]); mkdbg("carry0", [128, D], BF16); mkdbg("hfm", [NC, 128, D], BF16)
    mkdbg("v", [NC, 128, D], BF16); mkdbg("q", [NC, 128, D], BF16)
    mkdbg("carry_all", [NC, 128, D], BF16); mkdbg("y", [NC, 128, D], BF16)
    mkdbg("yhat", [NC, 128, D], BF16)
    mkdbg("wvT", [128, DC, D], BF16); mkdbg("wqT", [128, DC, D], BF16)
    mkdbg("x2", [T, D]); mkdbg("h2fm", [NBLK, 128, DC, FBLK], FP8)
    mkdbg("act", [NBLK, 128, KH, FBLK], FP8)

    from contextlib import ExitStack
    es = ExitStack()
    sb = lambda name, shape, dt: es.enter_context(nc.sbuf_tensor(name, shape, dt))

    identf = sb("identf", [128, 128], F32)
    identb = sb("identb", [128, 128], BF16)
    d_all = sb("d_all", [128, D], BF16)
    l3 = sb("l3", [128, 128], BF16)
    lsum = sb("lsum", [128, 128], BF16)
    mask16 = sb("mask16s", [128, D], BF16)
    lnwB = sb("lnwB", [128, D], BF16)
    lnbB = sb("lnbB", [128, D], BF16)
    gaB = sb("gaB", [128, D], BF16)
    carry = sb("carry", [128, D], BF16)
    modc = sb("modc", [128, 4 * DC], F32)
    mrealc = sb("mrealc", [128, NPRE], BF16)
    llB = sb("llB", [128, 128], F32)
    zcol = sb("zcol", [128, 1], F32)
    lamrow = sb("lamrow", [1, 128], F32)
    llrow = sb("llrow", [1, 128], F32)
    llcol = sb("llcol", [128, 1], F32)
    lam128 = sb("lam128", [128, 1], F32)
    onesr = sb("onesr", [1, 128], F32)
    bnst = [sb("bnst0", [128, 24], F32), sb("bnst1", [128, 24], F32)]
    mv = [sb("mv0", [128, 2], F32), sb("mv1", [128, 2], F32)]
    rstd = [sb("rstd0", [128, 1], F32), sb("rstd1", [128, 1], F32)]
    nmr = [sb("nmr0", [128, 1], F32), sb("nmr1", [128, 1], F32)]
    sigc = sb("sigc", [128, 1], F32)
    uft = sb("uft", [128, 128], F32)

    nc.const_aps.register(F32, 0.0, zcol[:, :])

    AB = 64 * 1024
    CB = 48 * 1024
    arA = nc.alloc_sbuf_tensor("arA", [128, AB], U8)
    arB = nc.alloc_sbuf_tensor("arB", [128, AB], U8)
    arC = nc.alloc_sbuf_tensor("arC", [128, CB], U8)
    bases = {"A": nc.lookup_mloc(arA).addr, "B": nc.lookup_mloc(arB).addr,
             "C": nc.lookup_mloc(arC).addr}

    def overlay(ar, off, name, shape, dtype):
        h = nc.virtual_sbuf_tensor(name, shape, dtype)
        nc._tensor(h.name, h.shape, h.dtype, type="SB")
        m = nc.lookup_mloc(h)
        m.allocated = True
        m.addr = bases[ar] + off
        return h

    genP = overlay("A", 0, "genP", [128, PRE], F32)
    genD = overlay("A", PRE * 4, "genD", [128, D], F32)
    rtmp = overlay("A", PRE * 4, "rtmp", [1, 2 * D], F32)
    off_genL = max(PRE * 4 + D * 4, 33 * 1024)
    genL = overlay("A", off_genL, "genL", [128, 128], F32)
    scvf = overlay("A", off_genL + 512, "scvf", [128, DC], BF16)
    wmst = overlay("A", 0, "wmst", [128, 2, DC, 512], BF16)
    wsA = overlay("A", 0, "wsA", [128, DC, D], BF16)
    act = overlay("A", 0, "act", [128, KH, FBLK], FP8)

    rows6 = overlay("B", 0, "rows6", [1, 6 * D], F32)
    lnwr = overlay("B", 6 * D * 4, "lnwr", [1, D], F32)
    lnbr = overlay("B", 6 * D * 4 + D * 4, "lnbr", [1, D], F32)
    wsB = overlay("B", 0, "wsB", [128, DC, D], BF16)
    h2fm = overlay("B", 0, "h2fm", [128, DC, FBLK], FP8)
    wg0b = overlay("B", 16 * 1024, "wg0b", [128, 2, DC, 128], FP8)
    wg1b = overlay("B", 20 * 1024, "wg1b", [128, 2, DC, 128], FP8)
    wlb = overlay("B", 24 * 1024, "wlb", [128, 2, KH, 256], FP8)
    gelb = overlay("B", 24 * 1024 + 2 * KH * 256, "gelb", [128, 2, FBLK], BF16)
    oev = overlay("B", 24 * 1024 + 2 * KH * 256 + 4 * FBLK, "oev", [128, 2, FBLK], BF16)

    xtf = overlay("C", 0, "xtf", [128, 2, D], F32)
    xn = overlay("C", 16 * 1024, "xn", [128, D], BF16)
    lpre = overlay("C", 20 * 1024, "lpre", [128, NPRE, 128], BF16)
    hfm = overlay("C", 20 * 1024, "hfm", [128, DC, 128], BF16)
    caB = overlay("C", 20 * 1024, "caB", [128, D], BF16)
    xpc = overlay("C", 20 * 1024, "xpc", [128, 2, 256], F32)
    outp = overlay("C", 22 * 1024, "outp", [128, 2, 256], F32)
    vtm = overlay("C", 24 * 1024, "vtm", [128, D], BF16)
    sgtm = overlay("C", 24 * 1024, "sgtm", [128, D], BF16)
    saB = overlay("C", 24 * 1024, "saB", [128, D], BF16)
    qtm = overlay("C", 28 * 1024, "qtm", [128, D], BF16)
    u_t = overlay("C", 28 * 1024, "u_t", [128, D], BF16)
    yhb = overlay("C", 32 * 1024, "yhb", [128, D], BF16)
    y2fm = overlay("C", 36 * 1024, "y2fm", [128, DC, 128], BF16)
    wstage = overlay("C", 36 * 1024, "wstage", [128, 2, D], BF16)
    gbB = overlay("C", 44 * 1024, "gbB", [128, D], BF16)

    pA = es.enter_context(nc.psum_tensor("pA", [128, D], F32))
    pB = es.enter_context(nc.psum_tensor("pB", [128, D], F32))
    pav = pA[:, :].bitcast(BF16)

    sems = {name: es.enter_context(nc.semaphore(name)) for name in SEMS}

    def rowap(r, off=0, n=None):
        n = D if n is None else n
        return rows6[0:1, r * D + off: r * D + off + n]

    def schedule(pp: P):
        t, v, s, g, y = pp.tensor, pp.vector, pp.scalar, pp.gpsimd, pp.sync
        inc, wait, at, dinc, waitm = pp.inc, pp.wait, pp.at, pp.dinc, pp.waitm
        marks = {}

        def ln_stats(src2d, ctx=0):
            nt = D // 512
            last = None
            for ii in range(nt):
                last = v.bn_stats(bnst[ctx][:, 6 * ii:6 * ii + 6],
                                  src2d[:, 512 * ii:512 * (ii + 1)])
            inc(last, "sV")
            wait("vector", "sV")
            mk = marks.get(("lnfree", ctx))
            if mk:
                wait("vector", "sS", mk)
            inc(v.bn_aggr(mv[ctx][:, :], bnst[ctx][:, 0:6 * nt]), "sV")

        def rstd_chain(ctx=0):
            wait("scalar", "sV")
            inc(s.activation(rstd[ctx][:, :], mv[ctx][:, 1:2], AF.Copy, bias=EPS,
                             scale=1.0), "sS")
            wait("scalar", "sS")
            inc(s.activation(rstd[ctx][:, :], rstd[ctx][:, :], AF.Ln), "sS")
            wait("scalar", "sS")
            inc(s.activation(rstd[ctx][:, :], rstd[ctx][:, :], AF.Exp, scale=-0.5), "sS")
            wait("scalar", "sS")
            inc(s.activation(nmr[ctx][:, :], mv[ctx][:, 0:1], AF.Identity,
                             scale=rstd[ctx][:, :]), "sS")
            wait("scalar", "sS")
            inc(s.mul(nmr[ctx][:, :], nmr[ctx][:, :], -1.0), "sS")
            wait("scalar", "sS")
            marks[("lnfree", ctx)] = at("sS")

        def normalize(dst, src, ctx=0):
            return inc(s.activation(dst, src, AF.Identity, bias=nmr[ctx][:, :],
                                    scale=rstd[ctx][:, :]), "sS")

        def dbgdma(out_ap, in_ap):
            mk = dinc(y.dma_start(out=out_ap, in_=in_ap), "dbx")
            waitm("vector", mk)
            waitm("scalar", mk)
            waitm("tensor", mk)
            waitm("gpsimd", mk)
            waitm("sync", mk)
            return mk

        def bcast_row(dst, src_row, addone=False):
            for bi in range(D // 512):
                bank = pA[:, (bi % 2) * 512:(bi % 2) * 512 + 512]
                if bi >= 2:
                    wait("tensor", "sV", marks["bc_ev"])
                mm = t.matmul(bank, onesr[0:1, 0:128], src_row[0:1, bi * 512:(bi + 1) * 512],
                              start=True, stop=True)
                inc(mm, "sT")
                wait("vector", "sT")
                if addone:
                    cp = v.tensor_scalar(dst[:, bi * 512:(bi + 1) * 512], bank, 1.0, None,
                                         ALU.add)
                else:
                    cp = v.tensor_copy(dst[:, bi * 512:(bi + 1) * 512], bank)
                marks["bc_ev"] = inc(cp, "sV")
            wait("tensor", "sV")

        def stage_weight(w_src, dst_slot, cast_dma, fold_gaB=False, tag="w",
                         first_wait_T=None, first_wait_dma=()):
            iss = "gpsimd" if cast_dma else "sync"
            se, so_ = ("wge", "wgo") if cast_dma else ("wse", "wso")
            for dc in range(DC):
                buf = dc % 2
                sem = se if buf == 0 else so_
                if dc >= 2:
                    wait(iss, "sT", marks[(tag + "t", dc - 2)])
                else:
                    if first_wait_T is not None:
                        wait(iss, "sT", first_wait_T)
                    for mk in first_wait_dma:
                        waitm(iss, mk)
                dma = (g.dma_start(out=wstage[:, buf, :], in_=w_src[dc * 128:(dc + 1) * 128, :])
                       if cast_dma else
                       y.dma_start(out=wstage[:, buf, :], in_=w_src[dc * 128:(dc + 1) * 128, :]))
                mk = dinc(dma, sem)
                waitm("tensor", mk)
                if dc >= 1:
                    wait("tensor", "sV", marks[(tag + "e", dc - 1)])
                last = None
                for k in range(DC):
                    last = t.transpose(pav[:, k * 128:(k + 1) * 128],
                                       wstage[:, buf, k * 128:(k + 1) * 128], identb[:, :])
                marks[(tag + "t", dc)] = inc(last, "sT")
                wait("vector", "sT")
                src3 = pav[:, 0:DC * 128].rearrange("p (k a) -> p k a", k=DC, a=128)
                if fold_gaB:
                    ga3 = bass.AP(gaB, dc * 128, [[D, 128], [0, DC], [1, 128]])
                    cp = v.tensor_tensor(dst_slot[:, :, dc * 128:(dc + 1) * 128], src3, ga3,
                                         ALU.mult)
                else:
                    cp = v.tensor_copy(dst_slot[:, :, dc * 128:(dc + 1) * 128], src3)
                marks[(tag + "e", dc)] = inc(cp, "sV")
            wait("tensor", "sV")
            return at("sV")

        for dst, src in ((identf[:, :], ident_in[:, :]), (mrealc[:, :], mreal_in[:, :]),
                         (mask16[:, :], mask16_in[:, :]), (lamrow[0:1, 0:H], beta_in[0:1, :]),
                         (lnwr[0:1, :], lnw_in[0:1, :]), (lnbr[0:1, :], lnb_in[0:1, :]),
                         (rows6[0:1, 0:D], cvec_in[0:1, :])):
            dinc(y.dma_start(out=dst, in_=src), "c0")
        marks["p0"] = ("c0", at("c0"))
        dinc(g.dma_start(out=identb[:, :], in_=ident_in[:, :]), "g0")
        dinc(g.dma_start(out=genD[:, :], in_=m_d_in[:, :]), "g0")
        dinc(g.dma_start(out=genP[:, :], in_=m_pre_in[:, :]), "g0")
        dinc(g.dma_start(out=genL[:, :], in_=m_l3_in[:, :]), "g0")
        marks["g0"] = ("g0", at("g0"))

        inc(v.memset(zcol[:, :], 0.0), "sV")
        inc(v.memset(onesr[:, :], 1.0), "sV")
        inc(v.memset(llrow[:, :], -1.0), "sV")
        marks["memsets"] = at("sV")

        waitm("scalar", marks["p0"])
        wait("scalar", "sV", marks["memsets"])
        inc(s.activation(lamrow[0:1, 0:H], lamrow[0:1, 0:H], AF.Sigmoid), "sS")
        wait("vector", "sS")
        inc(v.tensor_scalar(lamrow[0:1, 0:H], lamrow[0:1, 0:H], 1e-38, 1.0 - 1e-7,
                            ALU.max, ALU.min), "sV")
        if "lam" in dbg:
            wait("sync", "sV")
            dbgdma(dbg["lam"][0:1, :], lamrow[0:1, 0:H])
        wait("scalar", "sV")
        inc(s.activation(llrow[0:1, 0:H], lamrow[0:1, 0:H], AF.Ln), "sS")
        wait("tensor", "sS")
        waitm("tensor", marks["p0"])
        inc(t.transpose(pA[:, 0:1], llrow[0:1, :], identf[0:1, 0:1]), "sT")
        wait("vector", "sT")
        inc(v.tensor_copy(llcol[:, :], pA[:, 0:1]), "sV")
        wait("scalar", "sV")
        inc(s.activation(lam128[:, :], llcol[:, :], AF.Exp, scale=128.0), "sS")
        marks["lam128"] = at("sS")
        wait("tensor", "sV", marks["memsets"])
        inc(t.matmul(pA[:, 512:640], onesr[0:1, 0:128], llrow[0:1, 0:128],
                     start=True, stop=True), "sT")
        wait("vector", "sT")
        inc(v.tensor_copy(llB[:, :], pA[:, 512:640]), "sV")
        marks["llB"] = at("sV")

        waitm("vector", marks["g0"])
        wait("vector", "sV", marks["llB"])
        llDv = bass.AP(llB, 0, [[128, 128], [1, H], [0, 128]])
        inc(v.tensor_tensor(genD[:, :].rearrange("p (a b) -> p a b", a=H, b=128),
                            genD[:, :].rearrange("p (a b) -> p a b", a=H, b=128),
                            llDv, ALU.mult), "sV")
        wait("scalar", "sV")
        inc(s.activation(d_all[:, :], genD[:, :], AF.Exp), "sS")
        marks["d_all"] = at("sS")
        if "dall" in dbg:
            wait("sync", "sS")
            dbgdma(dbg["dall"][:, :], d_all[:, :])
        llPv = bass.AP(llB, 0, [[128, 128], [0, NPRE], [1, 128]])
        inc(v.tensor_tensor(genP[:, :].rearrange("p (a b) -> p a b", a=NPRE, b=128),
                            genP[:, :].rearrange("p (a b) -> p a b", a=NPRE, b=128),
                            llPv, ALU.mult), "sV")
        wait("scalar", "sV")
        inc(s.activation(lpre[:, :, :].rearrange("p a b -> p (a b)"), genP[:, :], AF.Exp), "sS")
        marks["lpre"] = at("sS")
        if "lpre" in dbg:
            wait("sync", "sS")
            dbgdma(dbg["lpre"][:, :], lpre[:, :, :].rearrange("p a b -> p (a b)"))
        wait("vector", "sS", marks["lam128"])
        inc(v.tensor_scalar(genL[:, :], genL[:, :], llcol[:, :], None, ALU.mult), "sV")
        wait("scalar", "sV")
        inc(s.activation(l3[:, :], genL[:, :], AF.Exp), "sS")
        marks["l3"] = at("sS")
        wait("gpsimd", "sS", marks["l3"])
        mk_msum = dinc(g.dma_start(out=genL[:, :], in_=m_sum_in[:, :]), "g0")
        waitm("vector", mk_msum)
        wait("vector", "sV", marks["llB"])
        inc(v.tensor_tensor(genL[:, :], genL[:, :], llB[:, :], ALU.mult), "sV")
        wait("scalar", "sV")
        inc(s.activation(lsum[:, :], genL[:, :], AF.Exp), "sS")
        marks["lsum"] = at("sS")

        waitm("scalar", marks["p0"])
        inc(s.activation(rows6[0:1, 0:D], rows6[0:1, 0:D], AF.Silu), "sS")
        wait("tensor", "sS")
        last = None
        for k in range(DC):
            last = t.transpose(pA[:, 1024 + k:1024 + k + 1],
                               rows6[0:1, k * 128:(k + 1) * 128], identf[0:1, 0:1])
        inc(last, "sT")
        wait("vector", "sT")
        inc(v.tensor_copy(scvf[:, :], pA[:, 1024:1024 + DC]), "sV")
        marks["scvf"] = at("sV")

        for it in range(max(NJ, NPRE)):
            if it < NJ:
                jb = it
                buf = jb % 2
                sem = "wme" if buf == 0 else "wmo"
                if jb == 0:
                    wait("gpsimd", "sS", marks["lpre"])
                if jb >= 2:
                    wait("gpsimd", "sT", marks[("wm_t", jb - 2)])
                src = bass.AP(wmod_in.tensor, jb * 512,
                              [[6 * D, 128], [128 * 6 * D, DC], [1, 512]])
                mk = dinc(g.dma_start(out=wmst[:, buf, :, :], in_=src), sem)
                waitm("tensor", mk)
                wait("tensor", "sV", marks["scvf"])
                if jb >= 1:
                    wait("tensor", "sV", marks[("wm_e", jb - 1)])
                last = None
                for k in range(DC):
                    last = t.matmul(pA[0:1, 1024:1536], scvf[:, k:k + 1], wmst[:, buf, k, :],
                                    start=(k == 0), stop=(k == DC - 1))
                marks[("wm_t", jb)] = inc(last, "sT")
                wait("vector", "sT")
                rr, cc = jb // NDB, (jb % NDB) * 512
                cp = v.tensor_copy(rows6[0:1, rr * D + cc:rr * D + cc + 512],
                                   pA[0:1, 1024:1536])
                marks[("wm_e", jb)] = inc(cp, "sV")
            if it < NPRE:
                pc = it
                buf = pc % 2
                sem = "xe" if buf == 0 else "xo"
                if pc >= 2:
                    wait("sync", "sS", marks[("pre_n", pc - 2)])
                mk = dinc(y.dma_start(out=xtf[:, buf, :],
                                      in_=xpre_in[pc * 128:(pc + 1) * 128, :]), sem)
                marks[("pre_ld", pc)] = mk
                waitm("vector", mk)
                ln_stats(xtf[:, buf, :])
                rstd_chain()
                marks[("pre_ch", pc)] = at("sS")
                if pc >= 1:
                    wait("scalar", "sT", marks[("pre_t", pc - 1)])
                marks[("pre_n", pc)] = normalize(xn[:, :], xtf[:, buf, :])
                wait("tensor", "sS")
                wait("tensor", "sS", marks["lpre"])
                waitm("tensor", marks["p0"])
                for db in range(NDB):
                    t.matmul(pB[:, db * 512:(db + 1) * 512], lpre[:, pc, :],
                             xn[:, db * 512:(db + 1) * 512],
                             start=(pc == 0), stop=(pc == NPRE - 1))
                last = t.matmul(pA[:, 1536:1537], lpre[:, pc, :], mrealc[:, pc:pc + 1],
                                start=(pc == 0), stop=(pc == NPRE - 1))
                marks[("pre_t", pc)] = inc(last, "sT")
        marks["mod_done"] = marks[("wm_e", NJ - 1)]
        marks["pre_done"] = marks[("pre_t", NPRE - 1)]
        if "modrow" in dbg:
            wait("sync", "sV", marks["mod_done"])
            dbgdma(dbg["modrow"][0:1, :], rows6[0:1, :])

        sgnr = rtmp[0:1, 0:D]
        absr = rtmp[0:1, D:2 * D]
        for r in (2, 5):
            wait("scalar", "sV", marks["mod_done"])
            wait("scalar", "sV")
            wait("scalar", "sS", marks["lsum"])
            inc(s.activation(sgnr, rowap(r), AF.Sign), "sS")
            wait("scalar", "sS")
            inc(s.activation(absr, rowap(r), AF.Abs), "sS")
            wait("vector", "sS")
            inc(v.tensor_scalar(absr, absr, 0.25, None, ALU.max), "sV")
            wait("vector", "sV")
            if "dbg_modrow" in marks:
                waitm("vector", marks["dbg_modrow"])
            inc(v.tensor_tensor(rowap(r), sgnr, absr, ALU.mult), "sV")
            marks[("gclamp", r)] = at("sV")

        wait("tensor", "sV")
        last = None
        for ci, r in enumerate((1, 0, 4, 3)):
            for k in range(DC):
                last = t.transpose(pA[:, 1024 + ci * DC + k:1024 + ci * DC + k + 1],
                                   rowap(r, k * 128, 128), identf[0:1, 0:1])
        inc(last, "sT")
        wait("vector", "sT")
        inc(v.tensor_copy(modc[:, :], pA[:, 1024:1024 + 4 * DC]), "sV")
        wait("vector", "sV")
        inc(v.tensor_scalar(modc[:, 0:DC], modc[:, 0:DC], 1.0, None, ALU.add), "sV")
        inc(v.tensor_scalar(modc[:, 2 * DC:3 * DC], modc[:, 2 * DC:3 * DC], 1.0, None,
                            ALU.add), "sV")
        marks["modc"] = at("sV")
        wait("tensor", "sV")

        marks["bc_ev"] = at("sV")
        bcast_row(lnwB[:, :], lnwr[0:1, :])
        bcast_row(lnbB[:, :], lnbr[0:1, :])
        wait("tensor", "sV", marks[("gclamp", 2)])
        bcast_row(gaB[:, :], rowap(2))
        wait("tensor", "sV", marks[("gclamp", 5)])
        bcast_row(gbB[:, :], rowap(5))
        bcast_row(caB[:, :], rowap(1), addone=True)
        bcast_row(saB[:, :], rowap(0))
        marks["btiles"] = at("sV")

        wait("vector", "sS", marks["lsum"])
        marks["wv_done"] = stage_weight(wv_in, wsA, True, tag="wv",
                                        first_wait_T=marks[("wm_t", NJ - 1)])
        if "wvT" in dbg:
            wait("sync", "sV", marks["wv_done"])
            dbgdma(dbg["wvT"][:, :, :], wsA[:, :, :])

        wait("vector", "sT", marks["pre_done"])
        wait("vector", "sV", marks["btiles"])
        inc(v.tensor_copy(sigc[:, :], pA[:, 1536:1537]), "sV")
        wait("vector", "sV")
        inc(v.tensor_tensor(pB[:, :], pB[:, :], caB[:, :], ALU.mult), "sV")
        wait("vector", "sV")
        inc(v.scalar_tensor_tensor(pB[:, :], saB[:, :], sigc[:, :], pB[:, :],
                                   ALU.mult, ALU.add), "sV")
        marks["ufold"] = at("sV")
        for k in range(DC):
            wait("vector", "sV", marks["ufold"])
            if k >= 1:
                wait("vector", "sT", marks[("ut_t", k - 1)])
            inc(v.tensor_copy(uft[:, :], pB[:, k * 128:(k + 1) * 128]), "sV")
            marks[("ut_c", k)] = at("sV")
            wait("tensor", "sV", marks[("ut_c", k)])
            if k >= 4:
                wait("tensor", "sV", marks[("ut_e", k - 4)])
            slot = k % 4
            marks[("ut_t", k)] = inc(
                t.transpose(pA[:, slot * 128:(slot + 1) * 128], uft[:, :], identf[:, :]), "sT")
            wait("vector", "sT", marks[("ut_t", k)])
            cp = v.tensor_copy(u_t[:, k * 128:(k + 1) * 128], pA[:, slot * 128:(slot + 1) * 128])
            marks[("ut_e", k)] = inc(cp, "sV")
        marks["u_t"] = at("sV")
        wait("tensor", "sV", marks["u_t"])
        last = None
        for db in range(NDB):
            for k in range(DC):
                last = t.matmul(pB[:, db * 512:(db + 1) * 512], u_t[:, k * 128:(k + 1) * 128],
                                wsA[:, k, db * 512:(db + 1) * 512],
                                start=(k == 0), stop=(k == DC - 1))
        inc(last, "sT")
        wait("vector", "sT")
        inc(v.tensor_tensor(carry[:, :], pB[:, :], mask16[:, :], ALU.mult), "sV")
        marks["carry"] = at("sV")
        if "carry0" in dbg:
            wait("sync", "sV", marks["carry"])
            dbgdma(dbg["carry0"][:, :], carry[:, :])

        wait("tensor", "sV", marks["carry"])
        marks["wq_done"] = stage_weight(wq_in, wsB, True, tag="wq",
                                        first_wait_T=marks[("wvt", DC - 1)])
        if "wqT" in dbg:
            wait("sync", "sV", marks["wq_done"])
            dbgdma(dbg["wqT"][:, :, :], wsB[:, :, :])

        def conv(dst, src, nrows, sem, step=256):
            for r0 in range(0, nrows, step):
                dinc(g.dma_start(out=dst[r0:r0 + step, :], in_=src[r0:r0 + step, :],
                                 max_dma_last_dim=8192), sem)
            return (sem, at(sem))
        conv(wg0f8, wg0_in, D, "cv1")
        marks["wgbf"] = conv(wgbf, wg_in, D, "cv1")
        conv(wg1f8, wg1_in, D, "cv2")
        marks["wobf"] = conv(wobf, wo_in, D, "cv2")
        marks["f8conv3"] = conv(wlf8, wl_in, HID, "cv3")

        def a_part2(c):
            wait("vector", "sV", marks[("a_y", c)])
            ln_stats(qtm[:, :], ctx=1)
            rstd_chain(ctx=1)
            marks[("a_ch", c)] = at("sS")
            if c >= 1:
                waitm("scalar", marks[("a_sy", c - 1)])
            marks[("a_yh", c)] = normalize(yhb[:, :], qtm[:, :], ctx=1)
            if "yhat" in dbg:
                dbgdma(dbg["yhat"][c, :, :], yhb[:, :])
            wait("sync", "sS", marks[("a_yh", c)])
            marks[("a_sy", c)] = dinc(y.dma_start(out=yhat_s[c * 128:(c + 1) * 128, :],
                                                  in_=yhb[:, :]), "ys")

        for c in range(NC):
            buf = c % 2
            sem = "xe" if buf == 0 else "xo"
            if c >= 2:
                wait("sync", "sS", marks[("a_n", c - 2)])
            else:
                wait("sync", "sS", marks[("pre_n", NPRE - 2 + c)])
            mk = dinc(y.dma_start(out=xtf[:, buf, :], in_=x_in[c * 128:(c + 1) * 128, :]), sem)
            waitm("vector", mk)
            ln_stats(xtf[:, buf, :])
            rstd_chain()
            if c >= 1:
                wait("scalar", "sT", marks[("a_ht", c - 1)])
            marks[("a_n", c)] = normalize(xn[:, :], xtf[:, buf, :])
            if c >= 1:
                a_part2(c - 1)
            wait("tensor", "sS", marks[("a_n", c)])
            if c >= 1:
                wait("tensor", "sV", marks[("a_y", c - 1)])
            last = None
            for k in range(DC):
                last = t.transpose(pav[:, k * 128:(k + 1) * 128], xn[:, k * 128:(k + 1) * 128],
                                   identb[:, :])
            marks[("a_ht", c)] = inc(last, "sT")
            wait("vector", "sT")
            wait("vector", "sV", marks["modc"])
            last = None
            for k in range(DC):
                last = v.tensor_scalar(hfm[:, k, :], pav[:, k * 128:(k + 1) * 128],
                                       modc[:, k:k + 1], modc[:, DC + k:DC + k + 1],
                                       ALU.mult, ALU.add)
            marks[("a_h", c)] = inc(last, "sV")
            if "hfm" in dbg:
                dbgdma(dbg["hfm"][c, :, :], hfm[:, :, :].rearrange("p k a -> p (k a)"))
            wait("tensor", "sV", marks[("a_h", c)])
            wait("tensor", "sV", marks.get(("a_carry", c - 1), marks["carry"]))
            last = None
            for k in range(DC):
                for db in range(NDB):
                    last = t.matmul(pB[:, db * 512:(db + 1) * 512], hfm[:, k, :],
                                    wsA[:, k, db * 512:(db + 1) * 512],
                                    start=(k == 0), stop=(k == DC - 1))
            inc(last, "sT")
            wait("vector", "sT")
            inc(v.tensor_copy(vtm[:, :], pB[:, :]), "sV")
            marks[("a_v", c)] = at("sV")
            if "v" in dbg:
                dbgdma(dbg["v"][c, :, :], vtm[:, :])
            wait("tensor", "sV", marks["wq_done"])
            last = None
            for k in range(DC):
                for db in range(NDB):
                    last = t.matmul(pA[:, db * 512:(db + 1) * 512], hfm[:, k, :],
                                    wsB[:, k, db * 512:(db + 1) * 512],
                                    start=(k == 0), stop=(k == DC - 1))
            inc(last, "sT")
            wait("vector", "sT")
            if c >= 1:
                wait("vector", "sS", marks[("a_yh", c - 1)])
            inc(v.tensor_copy(qtm[:, :], pA[:, :]), "sV")
            marks[("a_q", c)] = at("sV")
            if "q" in dbg:
                dbgdma(dbg["q"][c, :, :], qtm[:, :])
            wait("tensor", "sV", marks[("a_v", c)])
            last = None
            for db in range(NDB):
                last = t.matmul(pB[:, db * 512:(db + 1) * 512], lsum[:, :],
                                vtm[:, db * 512:(db + 1) * 512], start=True, stop=True)
            inc(last, "sT")
            wait("vector", "sT")
            inc(v.tensor_tensor(pB[:, :], pB[:, :], mask16[:, :], ALU.mult), "sV")
            marks[("a_l16", c)] = at("sV")
            wait("tensor", "sV", marks[("a_q", c)])
            last = None
            for b in range(NDB):
                for hh in range(4):
                    h = b * 4 + hh
                    last = t.matmul(pA[:, h * 128:(h + 1) * 128],
                                    d_all[:, h * 128:(h + 1) * 128],
                                    vtm[:, h * 128:(h + 1) * 128],
                                    start=(hh == 0), stop=False)
                last = t.matmul(pA[:, b * 512:(b + 1) * 512], l3[:, :],
                                carry[:, b * 512:(b + 1) * 512], start=False, stop=True)
            marks[("a_st", c)] = inc(last, "sT")
            wait("vector", "sT", marks[("a_st", c)])
            inc(v.tensor_tensor(qtm[:, :], qtm[:, :], pA[:, :], ALU.mult), "sV")
            marks[("a_y", c)] = at("sV")
            if "y" in dbg:
                dbgdma(dbg["y"][c, :, :], qtm[:, :])
            wait("vector", "sV", marks[("a_l16", c)])
            inc(v.scalar_tensor_tensor(carry[:, :], carry[:, :], lam128[:, :], pB[:, :],
                                       ALU.mult, ALU.add), "sV")
            marks[("a_carry", c)] = at("sV")
            if "carry_all" in dbg:
                dbgdma(dbg["carry_all"][c, :, :], carry[:, :])
        a_part2(NC - 1)
        marks["passA"] = at("sT")

        marks["wg_done"] = stage_weight(wgbf, wsA, False, tag="wg",
                                        first_wait_T=marks["passA"],
                                        first_wait_dma=(marks["wgbf"],))
        marks["wo_done"] = stage_weight(wobf, wsB, False, fold_gaB=True, tag="wo",
                                        first_wait_T=marks[("wgt", DC - 1)],
                                        first_wait_dma=(marks["wobf"],))

        for c in range(NC):
            buf = c % 2
            sem = "xe" if buf == 0 else "xo"
            if c >= 2:
                waitm("sync", marks[("b_x2", c - 2)])
            else:
                wait("sync", "sS", marks[("a_n", NC - 2 + c)])
            mkx = dinc(y.dma_start(out=xtf[:, buf, :], in_=x_in[c * 128:(c + 1) * 128, :]), sem)
            marks[("b_ldx", c)] = mkx
            if c >= 1:
                wait("sync", "sT", marks[("b_y2t", c - 1)])
            waitm("sync", marks[("a_sy", NC - 1)])
            waitm("sync", marks[("a_sy", c)])
            mky = dinc(y.dma_start(out=yhb[:, :], in_=yhat_s[c * 128:(c + 1) * 128, :]), "yl")
            marks[("b_ldy", c)] = mky
            waitm("vector", mkx)
            ln_stats(xtf[:, buf, :])
            rstd_chain()
            marks[("b_ch", c)] = at("sS")
            if c >= 1:
                wait("scalar", "sT", marks[("b_ht", c - 1)])
            else:
                wait("scalar", "sT", marks[("a_ht", NC - 1)])
            marks[("b_n", c)] = normalize(xn[:, :], xtf[:, buf, :])
            wait("tensor", "sS")
            if c >= 1:
                wait("tensor", "sV", marks[("b_y2fm", c - 1)])
            last = None
            for k in range(DC):
                last = t.transpose(pav[:, k * 128:(k + 1) * 128], xn[:, k * 128:(k + 1) * 128],
                                   identb[:, :])
            marks[("b_ht", c)] = inc(last, "sT")
            wait("vector", "sT")
            last = None
            for k in range(DC):
                last = v.tensor_scalar(hfm[:, k, :], pav[:, k * 128:(k + 1) * 128],
                                       modc[:, k:k + 1], modc[:, DC + k:DC + k + 1],
                                       ALU.mult, ALU.add)
            marks[("b_h", c)] = inc(last, "sV")
            wait("tensor", "sV", marks[("b_h", c)])
            wait("tensor", "sV", marks["wg_done"])
            if c >= 1:
                wait("tensor", "sV", marks[("b_r", c - 1)])
            last = None
            for k in range(DC):
                for db in range(NDB):
                    last = t.matmul(pB[:, db * 512:(db + 1) * 512], hfm[:, k, :],
                                    wsA[:, k, db * 512:(db + 1) * 512],
                                    start=(k == 0), stop=(k == DC - 1))
            marks[("b_g", c)] = inc(last, "sT")
            wait("scalar", "sT", marks[("b_g", c)])
            last = None
            for db in range(NDB):
                last = s.activation(sgtm[:, db * 512:(db + 1) * 512],
                                    pB[:, db * 512:(db + 1) * 512], AF.Silu)
            marks[("b_sg", c)] = inc(last, "sS")
            waitm("vector", marks[("b_ldy", c)])
            wait("vector", "sS", marks[("b_sg", c)])
            inc(v.tensor_tensor(yhb[:, :], yhb[:, :], lnwB[:, :], ALU.mult), "sV")
            wait("vector", "sV")
            inc(v.tensor_tensor(yhb[:, :], yhb[:, :], lnbB[:, :], ALU.add), "sV")
            wait("vector", "sV")
            inc(v.tensor_tensor(yhb[:, :], yhb[:, :], sgtm[:, :], ALU.mult), "sV")
            marks[("b_y2", c)] = at("sV")
            wait("tensor", "sV", marks[("b_y2", c)])
            wait("tensor", "sV", marks[("b_h", c)])
            last = None
            for k in range(DC):
                last = t.transpose(pav[:, k * 128:(k + 1) * 128], yhb[:, k * 128:(k + 1) * 128],
                                   identb[:, :])
            marks[("b_y2t", c)] = inc(last, "sT")
            wait("vector", "sT")
            src3 = pav[:, 0:DC * 128].rearrange("p (k a) -> p k a", k=DC, a=128)
            inc(v.tensor_copy(y2fm[:, :, :], src3), "sV")
            marks[("b_y2fm", c)] = at("sV")
            wait("tensor", "sV", marks[("b_y2fm", c)])
            wait("tensor", "sV", marks["wo_done"])
            last = None
            for k in range(DC):
                for db in range(NDB):
                    last = t.matmul(pB[:, db * 512:(db + 1) * 512], y2fm[:, k, :],
                                    wsB[:, k, db * 512:(db + 1) * 512],
                                    start=(k == 0), stop=(k == DC - 1))
            inc(last, "sT")
            wait("vector", "sT")
            inc(v.tensor_tensor(xtf[:, buf, :], pB[:, :], xtf[:, buf, :], ALU.add), "sV")
            marks[("b_r", c)] = at("sV")
            wait("sync", "sV", marks[("b_r", c)])
            sem2 = "x2se" if buf == 0 else "x2so"
            marks[("b_x2", c)] = dinc(y.dma_start(out=x2_s[c * 128:(c + 1) * 128, :],
                                                  in_=xtf[:, buf, :]), sem2)
            if "x2" in dbg:
                dbgdma(dbg["x2"][c * 128:(c + 1) * 128, :], xtf[:, buf, :])
        marks["passB"] = at("sV")

        for bi in range(NBLK):
            t0 = bi * FBLK
            for tt in range(FTC):
                buf = tt % 2
                sem = "xe" if ((bi * FTC + tt) % 2 == 0) else "xo"
                r0 = t0 + tt * 128
                if tt >= 2:
                    wait("sync", "sS", marks[("f_n", bi, tt - 2)])
                elif bi >= 1:
                    wait("sync", "sS", marks[("f_n", bi - 1, FTC - 2 + tt)])
                else:
                    waitm("sync", marks[("b_x2", NC - 2 + tt)])
                waitm("sync", marks[("b_x2", bi * FTC + tt)])
                mk = dinc(y.dma_start(out=xtf[:, buf, :], in_=x2_s[r0:r0 + 128, :]), sem)
                waitm("vector", mk)
                ln_stats(xtf[:, buf, :])
                rstd_chain()
                marks[("f_ch", bi, tt)] = at("sS")
                if tt >= 1:
                    wait("scalar", "sT", marks[("f_ht", bi, tt - 1)])
                elif bi >= 1:
                    wait("scalar", "sT", marks[("f_ht", bi - 1, FTC - 1)])
                else:
                    wait("scalar", "sT", marks[("b_ht", NC - 1)])
                marks[("f_n", bi, tt)] = normalize(xn[:, :], xtf[:, buf, :])
                wait("tensor", "sS")
                if tt >= 1:
                    wait("tensor", "sV", marks[("f_h", bi, tt - 1)])
                elif bi >= 1:
                    wait("tensor", "sV", marks[("f_res", bi - 1, NG - 1, FTC - 1)])
                else:
                    wait("tensor", "sV", marks[("b_y2fm", NC - 1)])
                last = None
                for k in range(DC):
                    last = t.transpose(pav[:, k * 128:(k + 1) * 128],
                                       xn[:, k * 128:(k + 1) * 128], identb[:, :])
                marks[("f_ht", bi, tt)] = inc(last, "sT")
                wait("vector", "sT")
                last = None
                for k in range(DC):
                    last = v.tensor_scalar(h2fm[:, k, tt * 128:(tt + 1) * 128],
                                           pav[:, k * 128:(k + 1) * 128],
                                           modc[:, 2 * DC + k:2 * DC + k + 1],
                                           modc[:, 3 * DC + k:3 * DC + k + 1],
                                           ALU.mult, ALU.add)
                marks[("f_h", bi, tt)] = inc(last, "sV")
            if "h2fm" in dbg:
                wait("sync", "sV", marks[("f_h", bi, FTC - 1)])
                dbgdma(dbg["h2fm"][bi, :, :, :], h2fm[:, :, :])

            waitm("scalar", marks["wgbf"])
            waitm("scalar", marks["wobf"])
            def _wgdma(kk):
                bb = kk % 2
                sem_ = "fge" if bb == 0 else "fgo"
                s0 = bass.AP(wg0f8.tensor, kk * 128, [[HID, 128], [128 * HID, DC], [1, 128]])
                s1 = bass.AP(wg1f8.tensor, kk * 128, [[HID, 128], [128 * HID, DC], [1, 128]])
                dinc(s.dma_start(out=wg0b[:, bb, :, :], in_=s0), sem_)
                marks[("f_wg", bi, kk)] = dinc(s.dma_start(out=wg1b[:, bb, :, :], in_=s1), sem_)
            if bi == 0:
                _wgdma(0)
                _wgdma(1)
            else:
                wait("scalar", "sT", marks[("f_mm", bi - 1, KH - 2)])
                _wgdma(0)
                wait("scalar", "sT", marks[("f_mm", bi - 1, KH - 1)])
                _wgdma(1)
            for k in range(KH):
                buf = k % 2
                ps = pA if buf == 0 else pB
                waitm("tensor", marks[("f_wg", bi, k)])
                if k >= 2:
                    wait("tensor", "sV", marks[("f_act", bi, k - 2)])
                    wait("tensor", "sS", marks[("f_gel", bi, k - 2)])
                elif k == 0:
                    wait("tensor", "sV", marks[("f_h", bi, FTC - 1)])
                    if bi >= 1:
                        wait("tensor", "sV", marks[("f_act", bi - 1, KH - 1)])
                        wait("tensor", "sS", marks[("f_gel", bi - 1, KH - 1)])
                else:
                    if bi >= 1:
                        wait("tensor", "sV", marks[("f_res", bi - 1, NG - 1, FTC - 1)])
                    else:
                        wait("tensor", "sV", marks[("b_r", NC - 1)])
                last = None
                for d in range(DC):
                    for hh in range(NH2):
                        last = t.matmul(ps[:, hh * 512:(hh + 1) * 512], wg0b[:, buf, d, :],
                                        h2fm[:, d, hh * 512:(hh + 1) * 512],
                                        start=(d == 0), stop=(d == DC - 1))
                        last = t.matmul(ps[:, FBLK + hh * 512:FBLK + (hh + 1) * 512],
                                        wg1b[:, buf, d, :],
                                        h2fm[:, d, hh * 512:(hh + 1) * 512],
                                        start=(d == 0), stop=(d == DC - 1))
                marks[("f_mm", bi, k)] = inc(last, "sT")
                wait("scalar", "sT", marks[("f_mm", bi, k)])
                if k + 2 < KH:
                    _wgdma(k + 2)
                if k >= 2:
                    wait("scalar", "sV", marks[("f_act", bi, k - 2)])
                elif bi >= 1:
                    wait("scalar", "sV", marks[("f_act", bi - 1, KH - 2 + k)])
                marks[("f_gel", bi, k)] = inc(
                    s.activation(gelb[:, buf, :], ps[:, 0:FBLK], AF.Gelu_apprx_tanh), "sS")
                wait("vector", "sS", marks[("f_gel", bi, k)])
                inc(v.tensor_tensor(act[:, k, :], gelb[:, buf, :], ps[:, FBLK:2 * FBLK],
                                    ALU.mult), "sV")
                marks[("f_act", bi, k)] = at("sV")
            if "act" in dbg:
                wait("sync", "sV", marks[("f_act", bi, KH - 1)])
                dbgdma(dbg["act"][bi, :, :, :], act[:, :, :])

            waitm("scalar", marks["f8conv3"])
            for gg in range(NG):
                buf = gg % 2
                ps = pA if buf == 0 else pB
                psv = ps[:, :].bitcast(BF16)
                sem = "fle" if buf == 0 else "flo"
                if gg >= 2:
                    wait("scalar", "sT", marks[("f_o", bi, gg - 2)])
                elif bi >= 1:
                    wait("scalar", "sT", marks[("f_o", bi - 1, NG - 2 + gg)])
                srcw = bass.AP(wlf8.tensor, gg * 256, [[D, 128], [128 * D, KH], [1, 256]])
                mkw = dinc(s.dma_start(out=wlb[:, buf, :, :], in_=srcw), sem)
                waitm("tensor", mkw)
                if gg >= 2:
                    wait("tensor", "sV", marks[("f_res", bi, gg - 2, FTC - 1)])
                elif gg == 0:
                    wait("tensor", "sV", marks[("f_act", bi, KH - 1)])
                    wait("tensor", "sS", marks[("f_gel", bi, KH - 1)])
                else:
                    wait("tensor", "sV", marks[("f_act", bi, KH - 2)])
                    wait("tensor", "sS", marks[("f_gel", bi, KH - 2)])
                last = None
                for k in range(KH):
                    for do2 in range(2):
                        for hh in range(NH2):
                            last = t.matmul(
                                ps[:, (do2 * NH2 + hh) * 512:(do2 * NH2 + hh) * 512 + 512],
                                wlb[:, buf, k, do2 * 128:(do2 + 1) * 128],
                                act[:, k, hh * 512:(hh + 1) * 512],
                                start=(k == 0), stop=(k == KH - 1))
                marks[("f_o", bi, gg)] = inc(last, "sT")
                wait("vector", "sT", marks[("f_o", bi, gg)])
                if gg >= 1:
                    wait("vector", "sT", marks[("f_t", bi, gg - 1)])
                elif bi >= 1:
                    wait("vector", "sT", marks[("f_t", bi - 1, NG - 1)])
                last = None
                for do2 in range(2):
                    last = v.tensor_copy(oev[:, do2, :], ps[:, do2 * FBLK:do2 * FBLK + FBLK])
                marks[("f_oe", bi, gg)] = inc(last, "sV")
                wait("tensor", "sV", marks[("f_oe", bi, gg)])
                last = None
                for tt in range(FTC):
                    for do2 in range(2):
                        last = t.transpose(
                            psv[:, (tt * 2 + do2) * 128:(tt * 2 + do2 + 1) * 128],
                            oev[:, do2, tt * 128:(tt + 1) * 128], identb[:, :])
                marks[("f_t", bi, gg)] = inc(last, "sT")
                for tt in range(FTC):
                    pbuf = tt % 2
                    semp = "xpe" if pbuf == 0 else "xpo"
                    semo = "ose" if pbuf == 0 else "oso"
                    r0 = t0 + tt * 128
                    c0 = gg * 256
                    if tt >= 2:
                        wait("sync", "sV", marks[("f_res", bi, gg, tt - 2)])
                    mkp = dinc(y.dma_start(out=xpc[:, pbuf, :],
                                           in_=x2_s[r0:r0 + 128, c0:c0 + 256]), semp)
                    waitm("vector", mkp)
                    wait("vector", "sT", marks[("f_t", bi, gg)])
                    if ("f_prev_out", pbuf) in marks:
                        waitm("vector", marks[("f_prev_out", pbuf)])
                    inc(v.tensor_tensor(outp[:, pbuf, :], psv[:, tt * 256:(tt + 1) * 256],
                                        gbB[:, c0:c0 + 256], ALU.mult), "sV")
                    wait("vector", "sV")
                    inc(v.tensor_tensor(outp[:, pbuf, :], outp[:, pbuf, :], xpc[:, pbuf, :],
                                        ALU.add), "sV")
                    marks[("f_res", bi, gg, tt)] = at("sV")
                    wait("sync", "sV", marks[("f_res", bi, gg, tt)])
                    marks[("f_prev_out", pbuf)] = dinc(
                        y.dma_start(out=out_t[r0:r0 + 128, c0:c0 + 256], in_=outp[:, pbuf, :]),
                        semo)

        for semname in DMA_SEMS:
            wait("sync", semname)

    with nc.Block() as block:
        @block.sync
        def _(e):
            schedule(P(nc, "sync", sems, e))
        @block.gpsimd
        def _(e):
            schedule(P(nc, "gpsimd", sems, e))
        @block.tensor
        def _(e):
            schedule(P(nc, "tensor", sems, e))
        @block.vector
        def _(e):
            schedule(P(nc, "vector", sems, e))
        @block.scalar
        def _(e):
            schedule(P(nc, "scalar", sems, e))

    return nc


def make_inmaps(cfg, inputs):
    D, H, T, PRE = cfg["D"], cfg["H"], cfg["T"], cfg["PRE"]
    NPRE = PRE // 128
    x = np.ascontiguousarray(np.asarray(inputs["L0_attn_x"], dtype=np.float32))
    B, L, _ = x.shape
    nslice = L // T
    consts = make_consts(cfg)
    wgat = np.asarray(inputs["L0_ffn_w_gating"], np.float32)
    shared = dict(
        wq=np.ascontiguousarray(np.asarray(inputs["L0_attn_Wq"], np.float32)),
        wv=np.ascontiguousarray(np.asarray(inputs["L0_attn_Wv"], np.float32)),
        wg=np.ascontiguousarray(np.asarray(inputs["L0_attn_Wg"], np.float32)),
        wo=np.ascontiguousarray(np.asarray(inputs["L0_attn_Wo"], np.float32)),
        wg0=np.ascontiguousarray(wgat[0]),
        wg1=np.ascontiguousarray(wgat[1]),
        wl=np.ascontiguousarray(np.asarray(inputs["L0_ffn_w_linear"], np.float32)),
        wmod=np.ascontiguousarray(np.asarray(inputs["L0_wmod"], np.float32)),
        beta=np.asarray(inputs["L0_attn_beta"], np.float32).reshape(1, H),
        lnw=np.asarray(inputs["L0_attn_ln_w"], np.float32).reshape(1, D),
        lnb=np.asarray(inputs["L0_attn_ln_b"], np.float32).reshape(1, D),
        m_d=consts["m_d"], m_pre=consts["m_pre"], m_l3=consts["m_l3"],
        m_sum=consts["m_sum"], mask16=consts["mask16"], ident=consts["ident"],
    )
    cv = np.asarray(inputs["L0_c"], np.float32)
    in_maps = []
    for core in range(B * nslice):
        b, sl = core // nslice, core % nslice
        xpre = np.zeros((PRE, D), np.float32)
        npre_tok = min(PRE, sl * T)
        if npre_tok:
            xpre[PRE - npre_tok:, :] = x[b, sl * T - npre_tok:sl * T, :]
        mreal = np.zeros((128, NPRE), ml_dtypes.bfloat16)
        nreal = npre_tok // 128
        if nreal:
            mreal[:, NPRE - nreal:] = 1.0
        m = dict(shared)
        m["x"] = np.ascontiguousarray(x[b, sl * T:(sl + 1) * T, :])
        m["xpre"] = xpre
        m["mreal"] = mreal
        m["cvec"] = np.ascontiguousarray(cv[b:b + 1, :])
        in_maps.append(m)
    return in_maps


def kernel(**inputs):
    cfg = full_cfg()
    D, T = cfg["D"], cfg["T"]
    x = np.asarray(inputs["L0_attn_x"])
    B, L, _ = x.shape
    nslice = L // T
    in_maps = make_inmaps(cfg, inputs)
    nc = build_program(cfg)
    res = run_bass_kernel_spmd(nc, in_maps, core_ids=list(range(B * nslice)))
    out = np.empty((B, L, D), np.float32)
    for core in range(B * nslice):
        b, sl = core // nslice, core % nslice
        out[b, sl * T:(sl + 1) * T, :] = res.results[core]["out"]
    return out


# version 06
# speedup vs baseline: 1.2752x; 1.2752x over previous
import os

os.environ.setdefault("GAUGE_NTFF_IGNORE_DMA", "1")

import numpy as np
import ml_dtypes

import concourse.bass as bass
import concourse.mybir as mybir
from concourse.bass_utils import run_bass_kernel_spmd

F32 = mybir.dt.float32
BF16 = mybir.dt.bfloat16
FP8 = mybir.dt.float8e4
U8 = mybir.dt.uint8
AF = mybir.ActivationFunctionType
ALU = mybir.AluOpType

BIG = 1.0e30
EPS = 1.0e-5


def full_cfg():
    return dict(D=2048, H=16, T=2048, PRE=6144, HID=8192, FBLK=1024)


def make_consts(cfg):
    D, H, PRE = cfg["D"], cfg["H"], cfg["PRE"]
    DH = D // H
    NPRE = PRE // 128
    j = np.arange(128)[:, None].astype(np.float32)
    i = np.arange(128)[None, :].astype(np.float32)

    blk = i - j
    blk[i < j] = BIG
    m_d = np.tile(blk, (1, H)).astype(np.float32)

    m_pre = np.full((128, NPRE * 128), BIG, np.float32)
    for c in range(NPRE):
        e = (PRE - 1 - (c * 128 + np.arange(128))).astype(np.float32)
        m_pre[:, c * 128:c * 128 + H] = e[:, None]

    m_l3 = np.full((128, 128), BIG, np.float32)
    m_l3[:H, :] = (np.arange(128) + 1.0)[None, :]

    m_sum = np.full((128, 128), BIG, np.float32)
    m_sum[:, :H] = (127.0 - np.arange(128))[:, None]

    mask16 = np.zeros((128, D), np.float32)
    for h in range(H):
        mask16[h, h * DH:(h + 1) * DH] = 1.0

    ident = np.eye(128, dtype=np.float32)
    return dict(m_d=m_d, m_pre=m_pre, m_l3=m_l3, m_sum=m_sum,
                mask16=mask16.astype(ml_dtypes.bfloat16), ident=ident)


class _Dummy:
    __slots__ = ()
    def __getattr__(self, name):
        return _dummy_call
def _dummy_call(*a, **k):
    return _DUMMY
_DUMMY = _Dummy()

ENGINES = ("sync", "gpsimd", "tensor", "vector", "scalar")
COMPUTE_SEMS = ("sT", "sV", "sS")
DMA_SEMS = ("c0", "g0", "dbx", "wme", "wmo", "xe", "xo", "ys", "yl",
            "x2se", "x2so", "ose", "oso", "xpe", "xpo", "wse", "wso",
            "wge", "wgo", "woe", "woo", "cv1", "cv2", "cv3", "fge", "fgo", "fle", "flo")
SEMS = COMPUTE_SEMS + DMA_SEMS


class P:
    def __init__(self, nc, cur, sems, real_engine):
        self.nc = nc
        self.cur = cur
        self.sems = sems
        self.cnt = {k: 0 for k in sems}
        for e in ENGINES:
            setattr(self, e, real_engine if e == cur else _DUMMY)

    def inc(self, inst, sem, n=1):
        self.cnt[sem] += n
        if inst is not _DUMMY:
            inst.then_inc(self.sems[sem], n)
        return self.cnt[sem]

    def dinc(self, inst, sem):
        v = self.inc(inst, sem, 16)
        return (sem, v)

    def wait(self, eng, sem, val=None):
        v = self.cnt[sem] if val is None else val
        if v > 0:
            getattr(self, eng).wait_ge(self.sems[sem], v)

    def waitm(self, eng, mark):
        self.wait(eng, mark[0], mark[1])

    def at(self, sem):
        return self.cnt[sem]


def build_program(cfg, debug_taps=()):
    D, H, T, PRE, HID, FBLK = (cfg[k] for k in ("D", "H", "T", "PRE", "HID", "FBLK"))
    DC = D // 128
    NC = T // 128
    NPRE = PRE // 128
    KH = HID // 128
    NBLK = T // FBLK
    FTC = FBLK // 128
    NDB = D // 512
    NJ = 6 * D // 512
    NH2 = FBLK // 512
    NG = NDB * 2

    nc = bass.Bass("TRN2", target_bir_lowering=False, detect_race_conditions=True)

    dram = lambda n, s, dt, kind: nc.dram_tensor(n, s, dt, kind=kind).ap()
    x_in = dram("x", [T, D], F32, "ExternalInput")
    xpre_in = dram("xpre", [PRE, D], F32, "ExternalInput")
    wq_in = dram("wq", [D, D], F32, "ExternalInput")
    wv_in = dram("wv", [D, D], F32, "ExternalInput")
    wg_in = dram("wg", [D, D], F32, "ExternalInput")
    wo_in = dram("wo", [D, D], F32, "ExternalInput")
    wg0_in = dram("wg0", [D, HID], F32, "ExternalInput")
    wg1_in = dram("wg1", [D, HID], F32, "ExternalInput")
    wl_in = dram("wl", [HID, D], F32, "ExternalInput")
    wmod_in = dram("wmod", [D, 6 * D], F32, "ExternalInput")
    cvec_in = dram("cvec", [1, D], F32, "ExternalInput")
    beta_in = dram("beta", [1, H], F32, "ExternalInput")
    lnw_in = dram("lnw", [1, D], F32, "ExternalInput")
    lnb_in = dram("lnb", [1, D], F32, "ExternalInput")
    m_d_in = dram("m_d", [128, D], F32, "ExternalInput")
    m_pre_in = dram("m_pre", [128, PRE], F32, "ExternalInput")
    m_l3_in = dram("m_l3", [128, 128], F32, "ExternalInput")
    m_sum_in = dram("m_sum", [128, 128], F32, "ExternalInput")
    mask16_in = dram("mask16", [128, D], BF16, "ExternalInput")
    ident_in = dram("ident", [128, 128], F32, "ExternalInput")
    mreal_in = dram("mreal", [128, NPRE], BF16, "ExternalInput")

    out_t = dram("out", [T, D], F32, "ExternalOutput")

    yhat_s = dram("yhat_s", [T, D], BF16, "Internal")
    x2_s = dram("x2_s", [T, D], F32, "Internal")
    wg0f8 = dram("wg0f8", [D, HID], FP8, "Internal")
    wg1f8 = dram("wg1f8", [D, HID], FP8, "Internal")
    wlf8 = dram("wlf8", [HID, D], FP8, "Internal")
    wgbf = dram("wgbf", [D, D], BF16, "Internal")
    wobf = dram("wobf", [D, D], BF16, "Internal")

    dbg = {}
    def mkdbg(name, shape, dtype=F32):
        if name in debug_taps:
            dbg[name] = dram("dbg_" + name, shape, dtype, "ExternalOutput")
    mkdbg("lam", [1, H]); mkdbg("dall", [128, D], BF16); mkdbg("lpre", [128, PRE], BF16)
    mkdbg("modrow", [1, 6 * D]); mkdbg("carry0", [128, D], BF16); mkdbg("hfm", [NC, 128, D], BF16)
    mkdbg("v", [NC, 128, D], BF16); mkdbg("q", [NC, 128, D], BF16)
    mkdbg("carry_all", [NC, 128, D], BF16); mkdbg("y", [NC, 128, D], BF16)
    mkdbg("yhat", [NC, 128, D], BF16)
    mkdbg("wvT", [128, DC, D], BF16); mkdbg("wqT", [128, DC, D], BF16)
    mkdbg("x2", [T, D]); mkdbg("h2fm", [NBLK, 128, DC, FBLK], FP8)
    mkdbg("act", [NBLK, 128, KH, FBLK], FP8)

    from contextlib import ExitStack
    es = ExitStack()
    sb = lambda name, shape, dt: es.enter_context(nc.sbuf_tensor(name, shape, dt))

    identf = sb("identf", [128, 128], F32)
    identb = sb("identb", [128, 128], BF16)
    d_all = sb("d_all", [128, D], BF16)
    l3 = sb("l3", [128, 128], BF16)
    lsum = sb("lsum", [128, 128], BF16)
    mask16 = sb("mask16s", [128, D], BF16)
    lnwB = sb("lnwB", [128, D], BF16)
    lnbB = sb("lnbB", [128, D], BF16)
    gaB = sb("gaB", [128, D], BF16)
    carry = sb("carry", [128, D], BF16)
    modc = sb("modc", [128, 4 * DC], F32)
    mrealc = sb("mrealc", [128, NPRE], BF16)
    llB = sb("llB", [128, 128], F32)
    zcol = sb("zcol", [128, 1], F32)
    lamrow = sb("lamrow", [1, 128], F32)
    llrow = sb("llrow", [1, 128], F32)
    llcol = sb("llcol", [128, 1], F32)
    lam128 = sb("lam128", [128, 1], F32)
    onesr = sb("onesr", [1, 128], F32)
    bnst = [sb("bnst0", [128, 24], F32), sb("bnst1", [128, 24], F32)]
    mv = [sb("mv0", [128, 2], F32), sb("mv1", [128, 2], F32)]
    rstd = [sb("rstd0", [128, 1], F32), sb("rstd1", [128, 1], F32)]
    nmr = [sb("nmr0", [128, 1], F32), sb("nmr1", [128, 1], F32)]
    sigc = sb("sigc", [128, 1], F32)
    uft = sb("uft", [128, 128], F32)

    nc.const_aps.register(F32, 0.0, zcol[:, :])

    AB = 64 * 1024
    CB = 48 * 1024
    arA = nc.alloc_sbuf_tensor("arA", [128, AB], U8)
    arB = nc.alloc_sbuf_tensor("arB", [128, AB], U8)
    arC = nc.alloc_sbuf_tensor("arC", [128, CB], U8)
    bases = {"A": nc.lookup_mloc(arA).addr, "B": nc.lookup_mloc(arB).addr,
             "C": nc.lookup_mloc(arC).addr}

    def overlay(ar, off, name, shape, dtype):
        h = nc.virtual_sbuf_tensor(name, shape, dtype)
        nc._tensor(h.name, h.shape, h.dtype, type="SB")
        m = nc.lookup_mloc(h)
        m.allocated = True
        m.addr = bases[ar] + off
        return h

    genP = overlay("A", 0, "genP", [128, PRE], F32)
    genD = overlay("A", PRE * 4, "genD", [128, D], F32)
    rtmp = overlay("A", PRE * 4, "rtmp", [1, 2 * D], F32)
    off_genL = max(PRE * 4 + D * 4, 33 * 1024)
    genL = overlay("A", off_genL, "genL", [128, 128], F32)
    scvf = overlay("A", off_genL + 512, "scvf", [128, DC], BF16)
    wmst = overlay("A", 0, "wmst", [128, 2, DC, 512], BF16)
    wsA = overlay("A", 0, "wsA", [128, DC, D], BF16)
    act = overlay("A", 0, "act", [128, KH, FBLK], FP8)

    rows6 = overlay("B", 0, "rows6", [1, 6 * D], F32)
    lnwr = overlay("B", 6 * D * 4, "lnwr", [1, D], F32)
    lnbr = overlay("B", 6 * D * 4 + D * 4, "lnbr", [1, D], F32)
    wsB = overlay("B", 0, "wsB", [128, DC, D], BF16)
    h2fm = overlay("B", 0, "h2fm", [128, DC, FBLK], FP8)
    wg0b = overlay("B", 16 * 1024, "wg0b", [128, 2, DC, 128], FP8)
    wg1b = overlay("B", 20 * 1024, "wg1b", [128, 2, DC, 128], FP8)
    wlb = overlay("B", 24 * 1024, "wlb", [128, 2, KH, 256], FP8)
    gelb = overlay("B", 24 * 1024 + 2 * KH * 256, "gelb", [128, 2, FBLK], BF16)
    oev = overlay("B", 24 * 1024 + 2 * KH * 256 + 4 * FBLK, "oev", [128, 2, FBLK], BF16)

    xtf = overlay("C", 0, "xtf", [128, 2, D], F32)
    xn0 = overlay("C", 16 * 1024, "xn0", [128, D], BF16)
    xn1 = overlay("C", 36 * 1024, "xn1", [128, D], BF16)
    lpre = overlay("C", 20 * 1024, "lpre", [128, NPRE, 128], BF16)
    hfm = overlay("C", 20 * 1024, "hfm", [128, DC, 128], BF16)
    caB = overlay("C", 20 * 1024, "caB", [128, D], BF16)
    xpc = overlay("C", 20 * 1024, "xpc", [128, 2, 256], F32)
    outp = overlay("C", 22 * 1024, "outp", [128, 2, 256], F32)
    vtm = overlay("C", 24 * 1024, "vtm", [128, D], BF16)
    sgtm = overlay("C", 24 * 1024, "sgtm", [128, D], BF16)
    saB = overlay("C", 24 * 1024, "saB", [128, D], BF16)
    qtm = overlay("C", 28 * 1024, "qtm", [128, D], BF16)
    u_t = overlay("C", 28 * 1024, "u_t", [128, D], BF16)
    y2fm = overlay("C", 28 * 1024, "y2fm", [128, DC, 128], BF16)
    yhb = overlay("C", 32 * 1024, "yhb", [128, D], BF16)
    wstage = overlay("C", 36 * 1024, "wstage", [128, 2, D], BF16)
    gbB = overlay("C", 44 * 1024, "gbB", [128, D], BF16)

    pA = es.enter_context(nc.psum_tensor("pA", [128, D], F32))
    pB = es.enter_context(nc.psum_tensor("pB", [128, D], F32))
    pav = pA[:, :].bitcast(BF16)

    sems = {name: es.enter_context(nc.semaphore(name)) for name in SEMS}

    def rowap(r, off=0, n=None):
        n = D if n is None else n
        return rows6[0:1, r * D + off: r * D + off + n]

    def schedule(pp: P):
        t, v, s, g, y = pp.tensor, pp.vector, pp.scalar, pp.gpsimd, pp.sync
        inc, wait, at, dinc, waitm = pp.inc, pp.wait, pp.at, pp.dinc, pp.waitm
        marks = {}

        def ln_stats(src2d, ctx=0):
            nt = D // 512
            last = None
            for ii in range(nt):
                last = v.bn_stats(bnst[ctx][:, 6 * ii:6 * ii + 6],
                                  src2d[:, 512 * ii:512 * (ii + 1)])
            inc(last, "sV")
            wait("vector", "sV")
            mk = marks.get(("lnfree", ctx))
            if mk:
                wait("vector", "sS", mk)
            inc(v.bn_aggr(mv[ctx][:, :], bnst[ctx][:, 0:6 * nt]), "sV")

        def rstd_chain(ctx=0):
            wait("scalar", "sV")
            inc(s.activation(rstd[ctx][:, :], mv[ctx][:, 1:2], AF.Copy, bias=EPS,
                             scale=1.0), "sS")
            wait("scalar", "sS")
            inc(s.activation(rstd[ctx][:, :], rstd[ctx][:, :], AF.Ln), "sS")
            wait("scalar", "sS")
            inc(s.activation(rstd[ctx][:, :], rstd[ctx][:, :], AF.Exp, scale=-0.5), "sS")
            wait("scalar", "sS")
            inc(s.activation(nmr[ctx][:, :], mv[ctx][:, 0:1], AF.Identity,
                             scale=rstd[ctx][:, :]), "sS")
            wait("scalar", "sS")
            inc(s.mul(nmr[ctx][:, :], nmr[ctx][:, :], -1.0), "sS")
            wait("scalar", "sS")
            marks[("lnfree", ctx)] = at("sS")

        def normalize(dst, src, ctx=0):
            return inc(s.activation(dst, src, AF.Identity, bias=nmr[ctx][:, :],
                                    scale=rstd[ctx][:, :]), "sS")

        def dbgdma(out_ap, in_ap):
            mk = dinc(y.dma_start(out=out_ap, in_=in_ap), "dbx")
            waitm("vector", mk)
            waitm("scalar", mk)
            waitm("tensor", mk)
            waitm("gpsimd", mk)
            waitm("sync", mk)
            return mk

        def bcast_row(dst, src_row, addone=False):
            for bi in range(D // 512):
                bank = pA[:, (bi % 2) * 512:(bi % 2) * 512 + 512]
                if bi >= 2:
                    wait("tensor", "sV", marks["bc_ev"])
                mm = t.matmul(bank, onesr[0:1, 0:128], src_row[0:1, bi * 512:(bi + 1) * 512],
                              start=True, stop=True)
                inc(mm, "sT")
                wait("vector", "sT")
                if addone:
                    cp = v.tensor_scalar(dst[:, bi * 512:(bi + 1) * 512], bank, 1.0, None,
                                         ALU.add)
                else:
                    cp = v.tensor_copy(dst[:, bi * 512:(bi + 1) * 512], bank)
                marks["bc_ev"] = inc(cp, "sV")
            wait("tensor", "sV")

        def stage_weight(w_src, dst_slot, cast_dma, fold_gaB=False, tag="w",
                         first_wait_T=None, first_wait_dma=()):
            iss = "gpsimd" if cast_dma else "sync"
            se, so_ = ("wge", "wgo") if cast_dma else ("wse", "wso")
            for dc in range(DC):
                buf = dc % 2
                sem = se if buf == 0 else so_
                if dc >= 2:
                    wait(iss, "sT", marks[(tag + "t", dc - 2)])
                else:
                    if first_wait_T is not None:
                        wait(iss, "sT", first_wait_T)
                    for mk in first_wait_dma:
                        waitm(iss, mk)
                dma = (g.dma_start(out=wstage[:, buf, :], in_=w_src[dc * 128:(dc + 1) * 128, :])
                       if cast_dma else
                       y.dma_start(out=wstage[:, buf, :], in_=w_src[dc * 128:(dc + 1) * 128, :]))
                mk = dinc(dma, sem)
                waitm("tensor", mk)
                if dc >= 1:
                    wait("tensor", "sV", marks[(tag + "e", dc - 1)])
                last = None
                for k in range(DC):
                    last = t.transpose(pav[:, k * 128:(k + 1) * 128],
                                       wstage[:, buf, k * 128:(k + 1) * 128], identb[:, :])
                marks[(tag + "t", dc)] = inc(last, "sT")
                wait("vector", "sT")
                src3 = pav[:, 0:DC * 128].rearrange("p (k a) -> p k a", k=DC, a=128)
                if fold_gaB:
                    ga3 = bass.AP(gaB, dc * 128, [[D, 128], [0, DC], [1, 128]])
                    cp = v.tensor_tensor(dst_slot[:, :, dc * 128:(dc + 1) * 128], src3, ga3,
                                         ALU.mult)
                else:
                    cp = v.tensor_copy(dst_slot[:, :, dc * 128:(dc + 1) * 128], src3)
                marks[(tag + "e", dc)] = inc(cp, "sV")
            wait("tensor", "sV")
            return at("sV")

        for dst, src in ((identf[:, :], ident_in[:, :]), (mrealc[:, :], mreal_in[:, :]),
                         (mask16[:, :], mask16_in[:, :]), (lamrow[0:1, 0:H], beta_in[0:1, :]),
                         (lnwr[0:1, :], lnw_in[0:1, :]), (lnbr[0:1, :], lnb_in[0:1, :]),
                         (rows6[0:1, 0:D], cvec_in[0:1, :])):
            dinc(y.dma_start(out=dst, in_=src), "c0")
        marks["p0"] = ("c0", at("c0"))
        dinc(g.dma_start(out=identb[:, :], in_=ident_in[:, :]), "g0")
        dinc(g.dma_start(out=genD[:, :], in_=m_d_in[:, :]), "g0")
        dinc(g.dma_start(out=genP[:, :], in_=m_pre_in[:, :]), "g0")
        dinc(g.dma_start(out=genL[:, :], in_=m_l3_in[:, :]), "g0")
        marks["g0"] = ("g0", at("g0"))

        inc(v.memset(zcol[:, :], 0.0), "sV")
        inc(v.memset(onesr[:, :], 1.0), "sV")
        inc(v.memset(llrow[:, :], -1.0), "sV")
        marks["memsets"] = at("sV")

        waitm("scalar", marks["p0"])
        wait("scalar", "sV", marks["memsets"])
        inc(s.activation(lamrow[0:1, 0:H], lamrow[0:1, 0:H], AF.Sigmoid), "sS")
        wait("vector", "sS")
        inc(v.tensor_scalar(lamrow[0:1, 0:H], lamrow[0:1, 0:H], 1e-38, 1.0 - 1e-7,
                            ALU.max, ALU.min), "sV")
        if "lam" in dbg:
            wait("sync", "sV")
            dbgdma(dbg["lam"][0:1, :], lamrow[0:1, 0:H])
        wait("scalar", "sV")
        inc(s.activation(llrow[0:1, 0:H], lamrow[0:1, 0:H], AF.Ln), "sS")
        wait("tensor", "sS")
        waitm("tensor", marks["p0"])
        inc(t.transpose(pA[:, 0:1], llrow[0:1, :], identf[0:1, 0:1]), "sT")
        wait("vector", "sT")
        inc(v.tensor_copy(llcol[:, :], pA[:, 0:1]), "sV")
        wait("scalar", "sV")
        inc(s.activation(lam128[:, :], llcol[:, :], AF.Exp, scale=128.0), "sS")
        marks["lam128"] = at("sS")
        wait("tensor", "sV", marks["memsets"])
        inc(t.matmul(pA[:, 512:640], onesr[0:1, 0:128], llrow[0:1, 0:128],
                     start=True, stop=True), "sT")
        wait("vector", "sT")
        inc(v.tensor_copy(llB[:, :], pA[:, 512:640]), "sV")
        marks["llB"] = at("sV")

        waitm("vector", marks["g0"])
        wait("vector", "sV", marks["llB"])
        llDv = bass.AP(llB, 0, [[128, 128], [1, H], [0, 128]])
        inc(v.tensor_tensor(genD[:, :].rearrange("p (a b) -> p a b", a=H, b=128),
                            genD[:, :].rearrange("p (a b) -> p a b", a=H, b=128),
                            llDv, ALU.mult), "sV")
        wait("scalar", "sV")
        inc(s.activation(d_all[:, :], genD[:, :], AF.Exp), "sS")
        marks["d_all"] = at("sS")
        if "dall" in dbg:
            wait("sync", "sS")
            dbgdma(dbg["dall"][:, :], d_all[:, :])
        llPv = bass.AP(llB, 0, [[128, 128], [0, NPRE], [1, 128]])
        inc(v.tensor_tensor(genP[:, :].rearrange("p (a b) -> p a b", a=NPRE, b=128),
                            genP[:, :].rearrange("p (a b) -> p a b", a=NPRE, b=128),
                            llPv, ALU.mult), "sV")
        wait("scalar", "sV")
        inc(s.activation(lpre[:, :, :].rearrange("p a b -> p (a b)"), genP[:, :], AF.Exp), "sS")
        marks["lpre"] = at("sS")
        if "lpre" in dbg:
            wait("sync", "sS")
            dbgdma(dbg["lpre"][:, :], lpre[:, :, :].rearrange("p a b -> p (a b)"))
        wait("vector", "sS", marks["lam128"])
        inc(v.tensor_scalar(genL[:, :], genL[:, :], llcol[:, :], None, ALU.mult), "sV")
        wait("scalar", "sV")
        inc(s.activation(l3[:, :], genL[:, :], AF.Exp), "sS")
        marks["l3"] = at("sS")
        wait("gpsimd", "sS", marks["l3"])
        mk_msum = dinc(g.dma_start(out=genL[:, :], in_=m_sum_in[:, :]), "g0")
        waitm("vector", mk_msum)
        wait("vector", "sV", marks["llB"])
        inc(v.tensor_tensor(genL[:, :], genL[:, :], llB[:, :], ALU.mult), "sV")
        wait("scalar", "sV")
        inc(s.activation(lsum[:, :], genL[:, :], AF.Exp), "sS")
        marks["lsum"] = at("sS")

        waitm("scalar", marks["p0"])
        inc(s.activation(rows6[0:1, 0:D], rows6[0:1, 0:D], AF.Silu), "sS")
        wait("tensor", "sS")
        last = None
        for k in range(DC):
            last = t.transpose(pA[:, 1024 + k:1024 + k + 1],
                               rows6[0:1, k * 128:(k + 1) * 128], identf[0:1, 0:1])
        inc(last, "sT")
        wait("vector", "sT")
        inc(v.tensor_copy(scvf[:, :], pA[:, 1024:1024 + DC]), "sV")
        marks["scvf"] = at("sV")

        for it in range(max(NJ, NPRE)):
            if it < NJ:
                jb = it
                buf = jb % 2
                sem = "wme" if buf == 0 else "wmo"
                if jb == 0:
                    wait("gpsimd", "sS", marks["lpre"])
                if jb >= 2:
                    wait("gpsimd", "sT", marks[("wm_t", jb - 2)])
                src = bass.AP(wmod_in.tensor, jb * 512,
                              [[6 * D, 128], [128 * 6 * D, DC], [1, 512]])
                mk = dinc(g.dma_start(out=wmst[:, buf, :, :], in_=src), sem)
                waitm("tensor", mk)
                wait("tensor", "sV", marks["scvf"])
                if jb >= 1:
                    wait("tensor", "sV", marks[("wm_e", jb - 1)])
                last = None
                for k in range(DC):
                    last = t.matmul(pA[0:1, 1024:1536], scvf[:, k:k + 1], wmst[:, buf, k, :],
                                    start=(k == 0), stop=(k == DC - 1))
                marks[("wm_t", jb)] = inc(last, "sT")
                wait("vector", "sT")
                rr, cc = jb // NDB, (jb % NDB) * 512
                cp = v.tensor_copy(rows6[0:1, rr * D + cc:rr * D + cc + 512],
                                   pA[0:1, 1024:1536])
                marks[("wm_e", jb)] = inc(cp, "sV")
            if it < NPRE:
                pc = it
                buf = pc % 2
                sem = "xe" if buf == 0 else "xo"
                if pc >= 2:
                    wait("sync", "sS", marks[("pre_n", pc - 2)])
                mk = dinc(y.dma_start(out=xtf[:, buf, :],
                                      in_=xpre_in[pc * 128:(pc + 1) * 128, :]), sem)
                marks[("pre_ld", pc)] = mk
                waitm("vector", mk)
                ln_stats(xtf[:, buf, :])
                rstd_chain()
                marks[("pre_ch", pc)] = at("sS")
                if pc >= 1:
                    wait("scalar", "sT", marks[("pre_t", pc - 1)])
                marks[("pre_n", pc)] = normalize(xn0[:, :], xtf[:, buf, :])
                wait("tensor", "sS")
                wait("tensor", "sS", marks["lpre"])
                waitm("tensor", marks["p0"])
                for db in range(NDB):
                    t.matmul(pB[:, db * 512:(db + 1) * 512], lpre[:, pc, :],
                             xn0[:, db * 512:(db + 1) * 512],
                             start=(pc == 0), stop=(pc == NPRE - 1))
                last = t.matmul(pA[:, 1536:1537], lpre[:, pc, :], mrealc[:, pc:pc + 1],
                                start=(pc == 0), stop=(pc == NPRE - 1))
                marks[("pre_t", pc)] = inc(last, "sT")
        marks["mod_done"] = marks[("wm_e", NJ - 1)]
        marks["pre_done"] = marks[("pre_t", NPRE - 1)]
        if "modrow" in dbg:
            wait("sync", "sV", marks["mod_done"])
            dbgdma(dbg["modrow"][0:1, :], rows6[0:1, :])

        sgnr = rtmp[0:1, 0:D]
        absr = rtmp[0:1, D:2 * D]
        for r in (2, 5):
            wait("scalar", "sV", marks["mod_done"])
            wait("scalar", "sV")
            wait("scalar", "sS", marks["lsum"])
            inc(s.activation(sgnr, rowap(r), AF.Sign), "sS")
            wait("scalar", "sS")
            inc(s.activation(absr, rowap(r), AF.Abs), "sS")
            wait("vector", "sS")
            inc(v.tensor_scalar(absr, absr, 0.25, None, ALU.max), "sV")
            wait("vector", "sV")
            if "dbg_modrow" in marks:
                waitm("vector", marks["dbg_modrow"])
            inc(v.tensor_tensor(rowap(r), sgnr, absr, ALU.mult), "sV")
            marks[("gclamp", r)] = at("sV")

        wait("tensor", "sV")
        last = None
        for ci, r in enumerate((1, 0, 4, 3)):
            for k in range(DC):
                last = t.transpose(pA[:, 1024 + ci * DC + k:1024 + ci * DC + k + 1],
                                   rowap(r, k * 128, 128), identf[0:1, 0:1])
        inc(last, "sT")
        wait("vector", "sT")
        inc(v.tensor_copy(modc[:, :], pA[:, 1024:1024 + 4 * DC]), "sV")
        wait("vector", "sV")
        inc(v.tensor_scalar(modc[:, 0:DC], modc[:, 0:DC], 1.0, None, ALU.add), "sV")
        inc(v.tensor_scalar(modc[:, 2 * DC:3 * DC], modc[:, 2 * DC:3 * DC], 1.0, None,
                            ALU.add), "sV")
        marks["modc"] = at("sV")
        wait("tensor", "sV")

        marks["bc_ev"] = at("sV")
        bcast_row(lnwB[:, :], lnwr[0:1, :])
        bcast_row(lnbB[:, :], lnbr[0:1, :])
        wait("tensor", "sV", marks[("gclamp", 2)])
        bcast_row(gaB[:, :], rowap(2))
        wait("tensor", "sV", marks[("gclamp", 5)])
        bcast_row(gbB[:, :], rowap(5))
        bcast_row(caB[:, :], rowap(1), addone=True)
        bcast_row(saB[:, :], rowap(0))
        marks["btiles"] = at("sV")

        wait("vector", "sS", marks["lsum"])
        marks["wv_done"] = stage_weight(wv_in, wsA, True, tag="wv",
                                        first_wait_T=marks[("wm_t", NJ - 1)])
        if "wvT" in dbg:
            wait("sync", "sV", marks["wv_done"])
            dbgdma(dbg["wvT"][:, :, :], wsA[:, :, :])

        wait("vector", "sT", marks["pre_done"])
        wait("vector", "sV", marks["btiles"])
        inc(v.tensor_copy(sigc[:, :], pA[:, 1536:1537]), "sV")
        wait("vector", "sV")
        inc(v.tensor_tensor(pB[:, :], pB[:, :], caB[:, :], ALU.mult), "sV")
        wait("vector", "sV")
        inc(v.scalar_tensor_tensor(pB[:, :], saB[:, :], sigc[:, :], pB[:, :],
                                   ALU.mult, ALU.add), "sV")
        marks["ufold"] = at("sV")
        for k in range(DC):
            wait("vector", "sV", marks["ufold"])
            if k >= 1:
                wait("vector", "sT", marks[("ut_t", k - 1)])
            inc(v.tensor_copy(uft[:, :], pB[:, k * 128:(k + 1) * 128]), "sV")
            marks[("ut_c", k)] = at("sV")
            wait("tensor", "sV", marks[("ut_c", k)])
            if k >= 4:
                wait("tensor", "sV", marks[("ut_e", k - 4)])
            slot = k % 4
            marks[("ut_t", k)] = inc(
                t.transpose(pA[:, slot * 128:(slot + 1) * 128], uft[:, :], identf[:, :]), "sT")
            wait("vector", "sT", marks[("ut_t", k)])
            cp = v.tensor_copy(u_t[:, k * 128:(k + 1) * 128], pA[:, slot * 128:(slot + 1) * 128])
            marks[("ut_e", k)] = inc(cp, "sV")
        marks["u_t"] = at("sV")
        wait("tensor", "sV", marks["u_t"])
        last = None
        for db in range(NDB):
            for k in range(DC):
                last = t.matmul(pB[:, db * 512:(db + 1) * 512], u_t[:, k * 128:(k + 1) * 128],
                                wsA[:, k, db * 512:(db + 1) * 512],
                                start=(k == 0), stop=(k == DC - 1))
        inc(last, "sT")
        wait("vector", "sT")
        inc(v.tensor_tensor(carry[:, :], pB[:, :], mask16[:, :], ALU.mult), "sV")
        marks["carry"] = at("sV")
        if "carry0" in dbg:
            wait("sync", "sV", marks["carry"])
            dbgdma(dbg["carry0"][:, :], carry[:, :])

        wait("tensor", "sV", marks["carry"])
        marks["wq_done"] = stage_weight(wq_in, wsB, True, tag="wq",
                                        first_wait_T=marks[("wvt", DC - 1)])
        if "wqT" in dbg:
            wait("sync", "sV", marks["wq_done"])
            dbgdma(dbg["wqT"][:, :, :], wsB[:, :, :])

        def conv(dst, src, nrows, sem, step=256):
            for r0 in range(0, nrows, step):
                dinc(g.dma_start(out=dst[r0:r0 + step, :], in_=src[r0:r0 + step, :],
                                 max_dma_last_dim=8192), sem)
            return (sem, at(sem))
        conv(wg0f8, wg0_in, D, "cv1")
        marks["wgbf"] = conv(wgbf, wg_in, D, "cv1")
        conv(wg1f8, wg1_in, D, "cv2")
        marks["wobf"] = conv(wobf, wo_in, D, "cv2")
        marks["f8conv3"] = conv(wlf8, wl_in, HID, "cv3")

        xns = (xn0, xn1)

        def a_front(c):
            buf = c % 2
            sem = "xe" if buf == 0 else "xo"
            if c >= 2:
                wait("sync", "sS", marks[("a_n", c - 2)])
            else:
                wait("sync", "sS", marks[("pre_n", NPRE - 2 + c)])
            mk = dinc(y.dma_start(out=xtf[:, buf, :], in_=x_in[c * 128:(c + 1) * 128, :]), sem)
            waitm("vector", mk)
            ln_stats(xtf[:, buf, :])
            rstd_chain()
            if c >= 2:
                wait("scalar", "sT", marks[("a_ht", c - 2)])
            elif c == 0:
                wait("scalar", "sT", marks[("pre_t", NPRE - 1)])
            else:
                wait("scalar", "sT", marks[("wqt", DC - 1)])
            marks[("a_n", c)] = normalize(xns[c % 2][:, :], xtf[:, buf, :])

        def a_part2(c):
            rstd_chain(ctx=1)
            marks[("a_ch", c)] = at("sS")
            if c >= 1:
                waitm("scalar", marks[("a_sy", c - 1)])
            marks[("a_yh", c)] = normalize(yhb[:, :], qtm[:, :], ctx=1)
            if "yhat" in dbg:
                dbgdma(dbg["yhat"][c, :, :], yhb[:, :])
            wait("sync", "sS", marks[("a_yh", c)])
            marks[("a_sy", c)] = dinc(y.dma_start(out=yhat_s[c * 128:(c + 1) * 128, :],
                                                  in_=yhb[:, :]), "ys")

        a_front(0)
        for c in range(NC):
            xnc = xns[c % 2]
            wait("tensor", "sS", marks[("a_n", c)])
            if c >= 1:
                wait("tensor", "sV", marks[("a_y", c - 1)])
            last = None
            for k in range(DC):
                last = t.transpose(pav[:, k * 128:(k + 1) * 128], xnc[:, k * 128:(k + 1) * 128],
                                   identb[:, :])
            marks[("a_ht", c)] = inc(last, "sT")
            wait("vector", "sT")
            wait("vector", "sV", marks["modc"])
            last = None
            for k in range(DC):
                last = v.tensor_scalar(hfm[:, k, :], pav[:, k * 128:(k + 1) * 128],
                                       modc[:, k:k + 1], modc[:, DC + k:DC + k + 1],
                                       ALU.mult, ALU.add)
            marks[("a_h", c)] = inc(last, "sV")
            if "hfm" in dbg:
                dbgdma(dbg["hfm"][c, :, :], hfm[:, :, :].rearrange("p k a -> p (k a)"))
            if c >= 1:
                wait("vector", "sV", marks[("a_y", c - 1)])
                ln_stats(qtm[:, :], ctx=1)
                a_part2(c - 1)
            if c + 1 < NC:
                a_front(c + 1)
            wait("tensor", "sV", marks[("a_h", c)])
            wait("tensor", "sV", marks.get(("a_carry", c - 1), marks["carry"]))
            last = None
            for k in range(DC):
                for db in range(NDB):
                    last = t.matmul(pB[:, db * 512:(db + 1) * 512], hfm[:, k, :],
                                    wsA[:, k, db * 512:(db + 1) * 512],
                                    start=(k == 0), stop=(k == DC - 1))
            marks[("a_vmm", c)] = inc(last, "sT")
            wait("vector", "sT")
            inc(v.tensor_copy(vtm[:, :], pB[:, :]), "sV")
            marks[("a_v", c)] = at("sV")
            if "v" in dbg:
                dbgdma(dbg["v"][c, :, :], vtm[:, :])
            wait("tensor", "sV", marks["wq_done"])
            last = None
            for k in range(DC):
                for db in range(NDB):
                    last = t.matmul(pA[:, db * 512:(db + 1) * 512], hfm[:, k, :],
                                    wsB[:, k, db * 512:(db + 1) * 512],
                                    start=(k == 0), stop=(k == DC - 1))
            marks[("a_qmm", c)] = inc(last, "sT")
            wait("vector", "sT")
            if c >= 1:
                wait("vector", "sS", marks[("a_yh", c - 1)])
            inc(v.tensor_copy(qtm[:, :], pA[:, :]), "sV")
            marks[("a_q", c)] = at("sV")
            if "q" in dbg:
                dbgdma(dbg["q"][c, :, :], qtm[:, :])
            wait("tensor", "sV", marks[("a_v", c)])
            last = None
            for db in range(NDB):
                last = t.matmul(pB[:, db * 512:(db + 1) * 512], lsum[:, :],
                                vtm[:, db * 512:(db + 1) * 512], start=True, stop=True)
            inc(last, "sT")
            wait("vector", "sT")
            inc(v.tensor_tensor(pB[:, :], pB[:, :], mask16[:, :], ALU.mult), "sV")
            marks[("a_l16", c)] = at("sV")
            wait("tensor", "sV", marks[("a_q", c)])
            last = None
            for b in range(NDB):
                for hh in range(4):
                    h = b * 4 + hh
                    last = t.matmul(pA[:, h * 128:(h + 1) * 128],
                                    d_all[:, h * 128:(h + 1) * 128],
                                    vtm[:, h * 128:(h + 1) * 128],
                                    start=(hh == 0), stop=False)
                last = t.matmul(pA[:, b * 512:(b + 1) * 512], l3[:, :],
                                carry[:, b * 512:(b + 1) * 512], start=False, stop=True)
            marks[("a_st", c)] = inc(last, "sT")
            wait("vector", "sT", marks[("a_st", c)])
            inc(v.tensor_tensor(qtm[:, :], qtm[:, :], pA[:, :], ALU.mult), "sV")
            marks[("a_y", c)] = at("sV")
            if "y" in dbg:
                dbgdma(dbg["y"][c, :, :], qtm[:, :])
            wait("vector", "sV", marks[("a_l16", c)])
            inc(v.scalar_tensor_tensor(carry[:, :], carry[:, :], lam128[:, :], pB[:, :],
                                       ALU.mult, ALU.add), "sV")
            marks[("a_carry", c)] = at("sV")
            if "carry_all" in dbg:
                dbgdma(dbg["carry_all"][c, :, :], carry[:, :])
        wait("vector", "sV", marks[("a_y", NC - 1)])
        ln_stats(qtm[:, :], ctx=1)
        a_part2(NC - 1)
        marks["passA"] = at("sT")

        for k in range(DC):
            sem = "wse" if k % 2 == 0 else "wso"
            if k == 0:
                waitm("sync", marks["wgbf"])
                wait("sync", "sT", marks[("a_vmm", NC - 1)])
            marks[("wgld", k)] = dinc(
                y.dma_start(out=wsA[:, k, :], in_=wgbf[:, k * 128:(k + 1) * 128],
                            transpose=True), sem)
        for k in range(DC):
            sem = "woe" if k % 2 == 0 else "woo"
            if k == 0:
                waitm("sync", marks["wobf"])
                wait("sync", "sT", marks[("a_qmm", NC - 1)])
            marks[("wold", k)] = dinc(
                y.dma_start(out=wsB[:, k, :], in_=wobf[:, k * 128:(k + 1) * 128],
                            transpose=True), sem)
        waitm("vector", marks[("wold", DC - 1)])
        waitm("vector", marks[("wold", DC - 2)])
        wait("vector", "sV", marks["btiles"])
        last = None
        for k in range(DC):
            last = v.tensor_tensor(wsB[:, k, :], wsB[:, k, :], gaB[:, :], ALU.mult)
        marks["wo_done"] = inc(last, "sV")

        def b_front(c):
            buf = c % 2
            sem = "xe" if buf == 0 else "xo"
            if c >= 2:
                waitm("sync", marks[("b_x2", c - 2)])
            else:
                wait("sync", "sS", marks[("a_n", NC - 2 + c)])
            mkx = dinc(y.dma_start(out=xtf[:, buf, :], in_=x_in[c * 128:(c + 1) * 128, :]), sem)
            marks[("b_ldx", c)] = mkx
            waitm("vector", mkx)
            ln_stats(xtf[:, buf, :])
            rstd_chain()
            marks[("b_ch", c)] = at("sS")
            if c >= 2:
                wait("scalar", "sT", marks[("b_ht", c - 2)])
            else:
                wait("scalar", "sT", marks[("a_ht", NC - 2 + c)])
            marks[("b_n", c)] = normalize(xns[c % 2][:, :], xtf[:, buf, :])

        b_front(0)
        for c in range(NC):
            buf = c % 2
            if c >= 1:
                wait("sync", "sT", marks[("b_y2t", c - 1)])
            waitm("sync", marks[("a_sy", NC - 1)])
            waitm("sync", marks[("a_sy", c)])
            mky = dinc(y.dma_start(out=yhb[:, :], in_=yhat_s[c * 128:(c + 1) * 128, :]), "yl")
            marks[("b_ldy", c)] = mky
            wait("tensor", "sS", marks[("b_n", c)])
            if c >= 1:
                wait("tensor", "sV", marks[("b_y2fm", c - 1)])
            last = None
            for k in range(DC):
                last = t.transpose(pav[:, k * 128:(k + 1) * 128],
                                   xns[c % 2][:, k * 128:(k + 1) * 128], identb[:, :])
            marks[("b_ht", c)] = inc(last, "sT")
            wait("vector", "sT")
            last = None
            for k in range(DC):
                last = v.tensor_scalar(hfm[:, k, :], pav[:, k * 128:(k + 1) * 128],
                                       modc[:, k:k + 1], modc[:, DC + k:DC + k + 1],
                                       ALU.mult, ALU.add)
            marks[("b_h", c)] = inc(last, "sV")
            if c + 1 < NC:
                b_front(c + 1)
            wait("tensor", "sV", marks[("b_h", c)])
            waitm("tensor", marks[("wgld", DC - 1)])
            waitm("tensor", marks[("wgld", DC - 2)])
            if c >= 1:
                wait("tensor", "sV", marks[("b_r", c - 1)])
            last = None
            for k in range(DC):
                for db in range(NDB):
                    last = t.matmul(pB[:, db * 512:(db + 1) * 512], hfm[:, k, :],
                                    wsA[:, k, db * 512:(db + 1) * 512],
                                    start=(k == 0), stop=(k == DC - 1))
            marks[("b_g", c)] = inc(last, "sT")
            wait("scalar", "sT", marks[("b_g", c)])
            last = None
            for db in range(NDB):
                last = s.activation(sgtm[:, db * 512:(db + 1) * 512],
                                    pB[:, db * 512:(db + 1) * 512], AF.Silu)
            marks[("b_sg", c)] = inc(last, "sS")
            waitm("vector", marks[("b_ldy", c)])
            wait("vector", "sS", marks[("b_sg", c)])
            inc(v.tensor_tensor(yhb[:, :], yhb[:, :], lnwB[:, :], ALU.mult), "sV")
            wait("vector", "sV")
            inc(v.tensor_tensor(yhb[:, :], yhb[:, :], lnbB[:, :], ALU.add), "sV")
            wait("vector", "sV")
            inc(v.tensor_tensor(yhb[:, :], yhb[:, :], sgtm[:, :], ALU.mult), "sV")
            marks[("b_y2", c)] = at("sV")
            wait("tensor", "sV", marks[("b_y2", c)])
            wait("tensor", "sV", marks[("b_h", c)])
            last = None
            for k in range(DC):
                last = t.transpose(pav[:, k * 128:(k + 1) * 128], yhb[:, k * 128:(k + 1) * 128],
                                   identb[:, :])
            marks[("b_y2t", c)] = inc(last, "sT")
            wait("vector", "sT")
            src3 = pav[:, 0:DC * 128].rearrange("p (k a) -> p k a", k=DC, a=128)
            inc(v.tensor_copy(y2fm[:, :, :], src3), "sV")
            marks[("b_y2fm", c)] = at("sV")
            wait("tensor", "sV", marks[("b_y2fm", c)])
            wait("tensor", "sV", marks["wo_done"])
            last = None
            for k in range(DC):
                for db in range(NDB):
                    last = t.matmul(pB[:, db * 512:(db + 1) * 512], y2fm[:, k, :],
                                    wsB[:, k, db * 512:(db + 1) * 512],
                                    start=(k == 0), stop=(k == DC - 1))
            inc(last, "sT")
            wait("vector", "sT")
            inc(v.tensor_tensor(xtf[:, buf, :], pB[:, :], xtf[:, buf, :], ALU.add), "sV")
            marks[("b_r", c)] = at("sV")
            wait("sync", "sV", marks[("b_r", c)])
            sem2 = "x2se" if buf == 0 else "x2so"
            marks[("b_x2", c)] = dinc(y.dma_start(out=x2_s[c * 128:(c + 1) * 128, :],
                                                  in_=xtf[:, buf, :]), sem2)
            if "x2" in dbg:
                dbgdma(dbg["x2"][c * 128:(c + 1) * 128, :], xtf[:, buf, :])
        marks["passB"] = at("sV")

        for bi in range(NBLK):
            t0 = bi * FBLK

            def f_front(tt, bi=None, t0=None):
                bi_, t0_ = bi, t0
                buf = tt % 2
                gidx = bi_ * FTC + tt
                sem = "xe" if gidx % 2 == 0 else "xo"
                r0 = t0_ + tt * 128
                if tt >= 2:
                    wait("sync", "sS", marks[("f_n", bi_, tt - 2)])
                elif bi_ >= 1:
                    wait("sync", "sS", marks[("f_n", bi_ - 1, FTC - 2 + tt)])
                else:
                    waitm("sync", marks[("b_x2", NC - 2 + tt)])
                waitm("sync", marks[("b_x2", gidx)])
                mk = dinc(y.dma_start(out=xtf[:, buf, :], in_=x2_s[r0:r0 + 128, :]), sem)
                waitm("vector", mk)
                ln_stats(xtf[:, buf, :])
                rstd_chain()
                marks[("f_ch", bi_, tt)] = at("sS")
                if tt >= 2:
                    wait("scalar", "sT", marks[("f_ht", bi_, tt - 2)])
                elif bi_ >= 1:
                    wait("scalar", "sT", marks[("f_ht", bi_ - 1, FTC - 2 + tt)])
                else:
                    wait("scalar", "sT", marks[("b_ht", NC - 2 + tt)])
                marks[("f_n", bi_, tt)] = normalize(xns[tt % 2][:, :], xtf[:, buf, :])

            f_front(0, bi=bi, t0=t0)
            for tt in range(FTC):
                wait("tensor", "sS", marks[("f_n", bi, tt)])
                if tt >= 1:
                    wait("tensor", "sV", marks[("f_h", bi, tt - 1)])
                elif bi >= 1:
                    wait("tensor", "sV", marks[("f_res", bi - 1, NG - 1, FTC - 1)])
                else:
                    wait("tensor", "sV", marks[("b_y2fm", NC - 1)])
                last = None
                for k in range(DC):
                    last = t.transpose(pav[:, k * 128:(k + 1) * 128],
                                       xns[tt % 2][:, k * 128:(k + 1) * 128], identb[:, :])
                marks[("f_ht", bi, tt)] = inc(last, "sT")
                wait("vector", "sT")
                last = None
                for k in range(DC):
                    last = v.tensor_scalar(h2fm[:, k, tt * 128:(tt + 1) * 128],
                                           pav[:, k * 128:(k + 1) * 128],
                                           modc[:, 2 * DC + k:2 * DC + k + 1],
                                           modc[:, 3 * DC + k:3 * DC + k + 1],
                                           ALU.mult, ALU.add)
                marks[("f_h", bi, tt)] = inc(last, "sV")
                if tt + 1 < FTC:
                    f_front(tt + 1, bi=bi, t0=t0)
            if "h2fm" in dbg:
                wait("sync", "sV", marks[("f_h", bi, FTC - 1)])
                dbgdma(dbg["h2fm"][bi, :, :, :], h2fm[:, :, :])

            waitm("scalar", marks["wgbf"])
            waitm("scalar", marks["wobf"])
            def _wgdma(kk):
                bb = kk % 2
                sem_ = "fge" if bb == 0 else "fgo"
                s0 = bass.AP(wg0f8.tensor, kk * 128, [[HID, 128], [128 * HID, DC], [1, 128]])
                s1 = bass.AP(wg1f8.tensor, kk * 128, [[HID, 128], [128 * HID, DC], [1, 128]])
                dinc(s.dma_start(out=wg0b[:, bb, :, :], in_=s0), sem_)
                marks[("f_wg", bi, kk)] = dinc(s.dma_start(out=wg1b[:, bb, :, :], in_=s1), sem_)
            if bi == 0:
                _wgdma(0)
                _wgdma(1)
            else:
                wait("scalar", "sT", marks[("f_mm", bi - 1, KH - 2)])
                _wgdma(0)
                wait("scalar", "sT", marks[("f_mm", bi - 1, KH - 1)])
                _wgdma(1)
            for k in range(KH):
                buf = k % 2
                ps = pA if buf == 0 else pB
                waitm("tensor", marks[("f_wg", bi, k)])
                if k >= 2:
                    wait("tensor", "sV", marks[("f_act", bi, k - 2)])
                    wait("tensor", "sS", marks[("f_gel", bi, k - 2)])
                elif k == 0:
                    wait("tensor", "sV", marks[("f_h", bi, FTC - 1)])
                    if bi >= 1:
                        wait("tensor", "sV", marks[("f_act", bi - 1, KH - 1)])
                        wait("tensor", "sS", marks[("f_gel", bi - 1, KH - 1)])
                else:
                    if bi >= 1:
                        wait("tensor", "sV", marks[("f_res", bi - 1, NG - 1, FTC - 1)])
                    else:
                        wait("tensor", "sV", marks[("b_r", NC - 1)])
                last = None
                for d in range(DC):
                    for hh in range(NH2):
                        last = t.matmul(ps[:, hh * 512:(hh + 1) * 512], wg0b[:, buf, d, :],
                                        h2fm[:, d, hh * 512:(hh + 1) * 512],
                                        start=(d == 0), stop=(d == DC - 1))
                        last = t.matmul(ps[:, FBLK + hh * 512:FBLK + (hh + 1) * 512],
                                        wg1b[:, buf, d, :],
                                        h2fm[:, d, hh * 512:(hh + 1) * 512],
                                        start=(d == 0), stop=(d == DC - 1))
                marks[("f_mm", bi, k)] = inc(last, "sT")
                wait("scalar", "sT", marks[("f_mm", bi, k)])
                if k + 2 < KH:
                    _wgdma(k + 2)
                if k >= 2:
                    wait("scalar", "sV", marks[("f_act", bi, k - 2)])
                elif bi >= 1:
                    wait("scalar", "sV", marks[("f_act", bi - 1, KH - 2 + k)])
                marks[("f_gel", bi, k)] = inc(
                    s.activation(gelb[:, buf, :], ps[:, 0:FBLK], AF.Gelu_apprx_tanh), "sS")
                wait("vector", "sS", marks[("f_gel", bi, k)])
                inc(v.tensor_tensor(act[:, k, :], gelb[:, buf, :], ps[:, FBLK:2 * FBLK],
                                    ALU.mult), "sV")
                marks[("f_act", bi, k)] = at("sV")
            if "act" in dbg:
                wait("sync", "sV", marks[("f_act", bi, KH - 1)])
                dbgdma(dbg["act"][bi, :, :, :], act[:, :, :])

            waitm("scalar", marks["f8conv3"])
            for gg in range(NG):
                buf = gg % 2
                ps = pA if buf == 0 else pB
                psv = ps[:, :].bitcast(BF16)
                sem = "fle" if buf == 0 else "flo"
                if gg >= 2:
                    wait("scalar", "sT", marks[("f_o", bi, gg - 2)])
                elif bi >= 1:
                    wait("scalar", "sT", marks[("f_o", bi - 1, NG - 2 + gg)])
                srcw = bass.AP(wlf8.tensor, gg * 256, [[D, 128], [128 * D, KH], [1, 256]])
                mkw = dinc(s.dma_start(out=wlb[:, buf, :, :], in_=srcw), sem)
                waitm("tensor", mkw)
                if gg >= 2:
                    wait("tensor", "sV", marks[("f_res", bi, gg - 2, FTC - 1)])
                elif gg == 0:
                    wait("tensor", "sV", marks[("f_act", bi, KH - 1)])
                    wait("tensor", "sS", marks[("f_gel", bi, KH - 1)])
                else:
                    wait("tensor", "sV", marks[("f_act", bi, KH - 2)])
                    wait("tensor", "sS", marks[("f_gel", bi, KH - 2)])
                last = None
                for k in range(KH):
                    for do2 in range(2):
                        for hh in range(NH2):
                            last = t.matmul(
                                ps[:, (do2 * NH2 + hh) * 512:(do2 * NH2 + hh) * 512 + 512],
                                wlb[:, buf, k, do2 * 128:(do2 + 1) * 128],
                                act[:, k, hh * 512:(hh + 1) * 512],
                                start=(k == 0), stop=(k == KH - 1))
                marks[("f_o", bi, gg)] = inc(last, "sT")
                wait("vector", "sT", marks[("f_o", bi, gg)])
                if gg >= 1:
                    wait("vector", "sT", marks[("f_t", bi, gg - 1)])
                elif bi >= 1:
                    wait("vector", "sT", marks[("f_t", bi - 1, NG - 1)])
                last = None
                for do2 in range(2):
                    last = v.tensor_copy(oev[:, do2, :], ps[:, do2 * FBLK:do2 * FBLK + FBLK])
                marks[("f_oe", bi, gg)] = inc(last, "sV")
                wait("tensor", "sV", marks[("f_oe", bi, gg)])
                last = None
                for tt in range(FTC):
                    for do2 in range(2):
                        last = t.transpose(
                            psv[:, (tt * 2 + do2) * 128:(tt * 2 + do2 + 1) * 128],
                            oev[:, do2, tt * 128:(tt + 1) * 128], identb[:, :])
                marks[("f_t", bi, gg)] = inc(last, "sT")
                for tt in range(FTC):
                    pbuf = tt % 2
                    semp = "xpe" if pbuf == 0 else "xpo"
                    semo = "ose" if pbuf == 0 else "oso"
                    r0 = t0 + tt * 128
                    c0 = gg * 256
                    if tt >= 2:
                        wait("sync", "sV", marks[("f_res", bi, gg, tt - 2)])
                    mkp = dinc(y.dma_start(out=xpc[:, pbuf, :],
                                           in_=x2_s[r0:r0 + 128, c0:c0 + 256]), semp)
                    waitm("vector", mkp)
                    wait("vector", "sT", marks[("f_t", bi, gg)])
                    if ("f_prev_out", pbuf) in marks:
                        waitm("vector", marks[("f_prev_out", pbuf)])
                    inc(v.tensor_tensor(outp[:, pbuf, :], psv[:, tt * 256:(tt + 1) * 256],
                                        gbB[:, c0:c0 + 256], ALU.mult), "sV")
                    wait("vector", "sV")
                    inc(v.tensor_tensor(outp[:, pbuf, :], outp[:, pbuf, :], xpc[:, pbuf, :],
                                        ALU.add), "sV")
                    marks[("f_res", bi, gg, tt)] = at("sV")
                    wait("sync", "sV", marks[("f_res", bi, gg, tt)])
                    marks[("f_prev_out", pbuf)] = dinc(
                        y.dma_start(out=out_t[r0:r0 + 128, c0:c0 + 256], in_=outp[:, pbuf, :]),
                        semo)

        for semname in DMA_SEMS:
            wait("sync", semname)

    with nc.Block() as block:
        @block.sync
        def _(e):
            schedule(P(nc, "sync", sems, e))
        @block.gpsimd
        def _(e):
            schedule(P(nc, "gpsimd", sems, e))
        @block.tensor
        def _(e):
            schedule(P(nc, "tensor", sems, e))
        @block.vector
        def _(e):
            schedule(P(nc, "vector", sems, e))
        @block.scalar
        def _(e):
            schedule(P(nc, "scalar", sems, e))

    return nc


def make_inmaps(cfg, inputs):
    D, H, T, PRE = cfg["D"], cfg["H"], cfg["T"], cfg["PRE"]
    NPRE = PRE // 128
    x = np.ascontiguousarray(np.asarray(inputs["L0_attn_x"], dtype=np.float32))
    B, L, _ = x.shape
    nslice = L // T
    consts = make_consts(cfg)
    wgat = np.asarray(inputs["L0_ffn_w_gating"], np.float32)
    shared = dict(
        wq=np.ascontiguousarray(np.asarray(inputs["L0_attn_Wq"], np.float32)),
        wv=np.ascontiguousarray(np.asarray(inputs["L0_attn_Wv"], np.float32)),
        wg=np.ascontiguousarray(np.asarray(inputs["L0_attn_Wg"], np.float32)),
        wo=np.ascontiguousarray(np.asarray(inputs["L0_attn_Wo"], np.float32)),
        wg0=np.ascontiguousarray(wgat[0]),
        wg1=np.ascontiguousarray(wgat[1]),
        wl=np.ascontiguousarray(np.asarray(inputs["L0_ffn_w_linear"], np.float32)),
        wmod=np.ascontiguousarray(np.asarray(inputs["L0_wmod"], np.float32)),
        beta=np.asarray(inputs["L0_attn_beta"], np.float32).reshape(1, H),
        lnw=np.asarray(inputs["L0_attn_ln_w"], np.float32).reshape(1, D),
        lnb=np.asarray(inputs["L0_attn_ln_b"], np.float32).reshape(1, D),
        m_d=consts["m_d"], m_pre=consts["m_pre"], m_l3=consts["m_l3"],
        m_sum=consts["m_sum"], mask16=consts["mask16"], ident=consts["ident"],
    )
    cv = np.asarray(inputs["L0_c"], np.float32)
    in_maps = []
    for core in range(B * nslice):
        b, sl = core // nslice, core % nslice
        xpre = np.zeros((PRE, D), np.float32)
        npre_tok = min(PRE, sl * T)
        if npre_tok:
            xpre[PRE - npre_tok:, :] = x[b, sl * T - npre_tok:sl * T, :]
        mreal = np.zeros((128, NPRE), ml_dtypes.bfloat16)
        nreal = npre_tok // 128
        if nreal:
            mreal[:, NPRE - nreal:] = 1.0
        m = dict(shared)
        m["x"] = np.ascontiguousarray(x[b, sl * T:(sl + 1) * T, :])
        m["xpre"] = xpre
        m["mreal"] = mreal
        m["cvec"] = np.ascontiguousarray(cv[b:b + 1, :])
        in_maps.append(m)
    return in_maps


def kernel(**inputs):
    cfg = full_cfg()
    D, T = cfg["D"], cfg["T"]
    x = np.asarray(inputs["L0_attn_x"])
    B, L, _ = x.shape
    nslice = L // T
    in_maps = make_inmaps(cfg, inputs)
    nc = build_program(cfg)
    res = run_bass_kernel_spmd(nc, in_maps, core_ids=list(range(B * nslice)))
    out = np.empty((B, L, D), np.float32)
    for core in range(B * nslice):
        b, sl = core // nslice, core % nslice
        out[b, sl * T:(sl + 1) * T, :] = res.results[core]["out"]
    return out


# version 07 (final)
# speedup vs baseline: 1.5503x; 1.2157x over previous
import os

os.environ.setdefault("GAUGE_NTFF_IGNORE_DMA", "1")

import numpy as np
import ml_dtypes

import concourse.bass as bass
import concourse.mybir as mybir
from concourse.bass_utils import run_bass_kernel_spmd

F32 = mybir.dt.float32
BF16 = mybir.dt.bfloat16
FP8 = mybir.dt.float8e4
U8 = mybir.dt.uint8
AF = mybir.ActivationFunctionType
ALU = mybir.AluOpType
PM = mybir.MatmulPerfMode

BIG = 1.0e30
EPS = 1.0e-5


def full_cfg():
    return dict(D=2048, H=16, T=2048, PRE=6144, HID=8192, FBLK=1024)


def make_consts(cfg):
    D, H, PRE = cfg["D"], cfg["H"], cfg["PRE"]
    DH = D // H
    NPRE = PRE // 128
    j = np.arange(128)[:, None].astype(np.float32)
    i = np.arange(128)[None, :].astype(np.float32)

    blk = i - j
    blk[i < j] = BIG
    m_d = np.tile(blk, (1, H)).astype(np.float32)

    m_pre = np.full((128, NPRE * 128), BIG, np.float32)
    for c in range(NPRE):
        e = (PRE - 1 - (c * 128 + np.arange(128))).astype(np.float32)
        m_pre[:, c * 128:c * 128 + H] = e[:, None]

    m_l3 = np.full((128, 128), BIG, np.float32)
    m_l3[:H, :] = (np.arange(128) + 1.0)[None, :]

    m_sum = np.full((128, 128), BIG, np.float32)
    m_sum[:, :H] = (127.0 - np.arange(128))[:, None]

    mask16 = np.zeros((128, D), np.float32)
    for h in range(H):
        mask16[h, h * DH:(h + 1) * DH] = 1.0

    ident = np.eye(128, dtype=np.float32)
    return dict(m_d=m_d, m_pre=m_pre, m_l3=m_l3, m_sum=m_sum,
                mask16=mask16.astype(ml_dtypes.bfloat16), ident=ident)


class _Dummy:
    __slots__ = ()
    def __getattr__(self, name):
        return _dummy_call
def _dummy_call(*a, **k):
    return _DUMMY
_DUMMY = _Dummy()

ENGINES = ("sync", "gpsimd", "tensor", "vector", "scalar")
COMPUTE_SEMS = ("sT", "sV", "sS")
DMA_SEMS = ("c0", "g0", "dbx", "wme", "wmo", "xe", "xo", "ys", "yl",
            "x2se", "x2so", "ose", "oso", "xpe", "xpo", "wse", "wso",
            "wge", "wgo", "woe", "woo", "cv1", "cv2", "cv3", "fge", "fgo", "fle", "flo")
SEMS = COMPUTE_SEMS + DMA_SEMS


class P:
    def __init__(self, nc, cur, sems, real_engine):
        self.nc = nc
        self.cur = cur
        self.sems = sems
        self.cnt = {k: 0 for k in sems}
        for e in ENGINES:
            setattr(self, e, real_engine if e == cur else _DUMMY)

    def inc(self, inst, sem, n=1):
        self.cnt[sem] += n
        if inst is not _DUMMY:
            inst.then_inc(self.sems[sem], n)
        return self.cnt[sem]

    def dinc(self, inst, sem):
        v = self.inc(inst, sem, 16)
        return (sem, v)

    def wait(self, eng, sem, val=None):
        v = self.cnt[sem] if val is None else val
        if v > 0:
            getattr(self, eng).wait_ge(self.sems[sem], v)

    def waitm(self, eng, mark):
        self.wait(eng, mark[0], mark[1])

    def at(self, sem):
        return self.cnt[sem]


def build_program(cfg, debug_taps=()):
    D, H, T, PRE, HID, FBLK = (cfg[k] for k in ("D", "H", "T", "PRE", "HID", "FBLK"))
    DC = D // 128
    NC = T // 128
    NPRE = PRE // 128
    KH = HID // 128
    NBLK = T // FBLK
    FTC = FBLK // 128
    NDB = D // 512
    NJ = 6 * D // 512
    NH2 = FBLK // 512
    NG = NDB * 2

    nc = bass.Bass("TRN2", target_bir_lowering=False, detect_race_conditions=True)

    dram = lambda n, s, dt, kind: nc.dram_tensor(n, s, dt, kind=kind).ap()
    x_in = dram("x", [T, D], F32, "ExternalInput")
    xpre_in = dram("xpre", [PRE, D], F32, "ExternalInput")
    wq_in = dram("wq", [D, D], F32, "ExternalInput")
    wv_in = dram("wv", [D, D], F32, "ExternalInput")
    wg_in = dram("wg", [D, D], F32, "ExternalInput")
    wo_in = dram("wo", [D, D], F32, "ExternalInput")
    wg0_in = dram("wg0", [D, HID], F32, "ExternalInput")
    wg1_in = dram("wg1", [D, HID], F32, "ExternalInput")
    wl_in = dram("wl", [HID, D], F32, "ExternalInput")
    wmod_in = dram("wmod", [D, 6 * D], F32, "ExternalInput")
    cvec_in = dram("cvec", [1, D], F32, "ExternalInput")
    beta_in = dram("beta", [1, H], F32, "ExternalInput")
    lnw_in = dram("lnw", [1, D], F32, "ExternalInput")
    lnb_in = dram("lnb", [1, D], F32, "ExternalInput")
    m_d_in = dram("m_d", [128, D], F32, "ExternalInput")
    m_pre_in = dram("m_pre", [128, PRE], F32, "ExternalInput")
    m_l3_in = dram("m_l3", [128, 128], F32, "ExternalInput")
    m_sum_in = dram("m_sum", [128, 128], F32, "ExternalInput")
    mask16_in = dram("mask16", [128, D], BF16, "ExternalInput")
    ident_in = dram("ident", [128, 128], F32, "ExternalInput")
    mreal_in = dram("mreal", [128, NPRE], BF16, "ExternalInput")

    out_t = dram("out", [T, D], F32, "ExternalOutput")

    yhat_s = dram("yhat_s", [T, D], BF16, "Internal")
    x2_s = dram("x2_s", [T, D], F32, "Internal")
    wg0f8 = dram("wg0f8", [D, HID], FP8, "Internal")
    wg1f8 = dram("wg1f8", [D, HID], FP8, "Internal")
    wlf8 = dram("wlf8", [HID, D], FP8, "Internal")
    wgbf = dram("wgbf", [D, D], BF16, "Internal")
    wobf = dram("wobf", [D, D], BF16, "Internal")

    dbg = {}
    def mkdbg(name, shape, dtype=F32):
        if name in debug_taps:
            dbg[name] = dram("dbg_" + name, shape, dtype, "ExternalOutput")
    mkdbg("lam", [1, H]); mkdbg("dall", [128, D], BF16); mkdbg("lpre", [128, PRE], BF16)
    mkdbg("modrow", [1, 6 * D]); mkdbg("carry0", [128, D], BF16); mkdbg("hfm", [NC, 128, D], BF16)
    mkdbg("v", [NC, 128, D], BF16); mkdbg("q", [NC, 128, D], BF16)
    mkdbg("carry_all", [NC, 128, D], BF16); mkdbg("y", [NC, 128, D], BF16)
    mkdbg("yhat", [NC, 128, D], BF16)
    mkdbg("wvT", [128, DC, D], BF16); mkdbg("wqT", [128, DC, D], BF16)
    mkdbg("x2", [T, D]); mkdbg("h2fm", [NBLK, 128, DC, FBLK], FP8)
    mkdbg("act", [NBLK, 128, KH, FBLK], FP8)

    from contextlib import ExitStack
    es = ExitStack()
    sb = lambda name, shape, dt: es.enter_context(nc.sbuf_tensor(name, shape, dt))

    identf = sb("identf", [128, 128], F32)
    identb = sb("identb", [128, 128], BF16)
    d_all = sb("d_all", [128, D], BF16)
    l3 = sb("l3", [128, 128], BF16)
    lsum = sb("lsum", [128, 128], BF16)
    mask16 = sb("mask16s", [128, D], BF16)
    lnwB = sb("lnwB", [128, D], BF16)
    lnbB = sb("lnbB", [128, D], BF16)
    gaB = sb("gaB", [128, D], BF16)
    carry = sb("carry", [128, D], BF16)
    modc = sb("modc", [128, 4 * DC], F32)
    mrealc = sb("mrealc", [128, NPRE], BF16)
    llB = sb("llB", [128, 128], F32)
    zcol = sb("zcol", [128, 1], F32)
    lamrow = sb("lamrow", [1, 128], F32)
    llrow = sb("llrow", [1, 128], F32)
    llcol = sb("llcol", [128, 1], F32)
    lam128 = sb("lam128", [128, 1], F32)
    onesr = sb("onesr", [1, 128], F32)
    bnst = [sb("bnst0", [128, 24], F32), sb("bnst1", [128, 24], F32)]
    mv = [sb("mv0", [128, 2], F32), sb("mv1", [128, 2], F32)]
    rstd = [sb("rstd0", [128, 1], F32), sb("rstd1", [128, 1], F32)]
    nmr = [sb("nmr0", [128, 1], F32), sb("nmr1", [128, 1], F32)]
    sigc = sb("sigc", [128, 1], F32)
    uft = sb("uft", [128, 128], F32)

    nc.const_aps.register(F32, 0.0, zcol[:, :])

    AB = 64 * 1024
    CB = 48 * 1024
    arA = nc.alloc_sbuf_tensor("arA", [128, AB], U8)
    arB = nc.alloc_sbuf_tensor("arB", [128, AB], U8)
    arC = nc.alloc_sbuf_tensor("arC", [128, CB], U8)
    bases = {"A": nc.lookup_mloc(arA).addr, "B": nc.lookup_mloc(arB).addr,
             "C": nc.lookup_mloc(arC).addr}

    def overlay(ar, off, name, shape, dtype):
        h = nc.virtual_sbuf_tensor(name, shape, dtype)
        nc._tensor(h.name, h.shape, h.dtype, type="SB")
        m = nc.lookup_mloc(h)
        m.allocated = True
        m.addr = bases[ar] + off
        return h

    genP = overlay("A", 0, "genP", [128, PRE], F32)
    genD = overlay("A", PRE * 4, "genD", [128, D], F32)
    rtmp = overlay("A", PRE * 4, "rtmp", [1, 2 * D], F32)
    off_genL = max(PRE * 4 + D * 4, 33 * 1024)
    genL = overlay("A", off_genL, "genL", [128, 128], F32)
    scvf = overlay("A", off_genL + 512, "scvf", [128, DC], BF16)
    wmst = overlay("A", 0, "wmst", [128, 2, DC, 512], BF16)
    wsA = overlay("A", 0, "wsA", [128, DC, D], BF16)
    act = overlay("A", 0, "act", [128, KH, FBLK], FP8)

    rows6 = overlay("B", 0, "rows6", [1, 6 * D], F32)
    lnwr = overlay("B", 6 * D * 4, "lnwr", [1, D], F32)
    lnbr = overlay("B", 6 * D * 4 + D * 4, "lnbr", [1, D], F32)
    wsB = overlay("B", 0, "wsB", [128, DC, D], BF16)
    h2fm = overlay("B", 0, "h2fm", [128, DC, FBLK], FP8)
    wg0b = overlay("B", 16 * 1024, "wg0b", [128, 2, DC, 128], FP8)
    wg1b = overlay("B", 20 * 1024, "wg1b", [128, 2, DC, 128], FP8)
    wlb = overlay("B", 24 * 1024, "wlb", [128, 2, KH, 256], FP8)
    gelb = overlay("B", 24 * 1024 + 2 * KH * 256, "gelb", [128, 2, FBLK], BF16)
    oev = overlay("B", 24 * 1024 + 2 * KH * 256 + 4 * FBLK, "oev", [128, 2, FBLK], BF16)

    xtf = overlay("C", 0, "xtf", [128, 2, D], F32)
    xn0 = overlay("C", 16 * 1024, "xn0", [128, D], BF16)
    xn1 = overlay("C", 36 * 1024, "xn1", [128, D], BF16)
    lpre = overlay("C", 20 * 1024, "lpre", [128, NPRE, 128], BF16)
    hfm = overlay("C", 20 * 1024, "hfm", [128, DC, 128], BF16)
    caB = overlay("C", 20 * 1024, "caB", [128, D], BF16)
    xpc = overlay("C", 20 * 1024, "xpc", [128, 2, 256], F32)
    outp = overlay("C", 22 * 1024, "outp", [128, 2, 256], F32)
    vtm = overlay("C", 24 * 1024, "vtm", [128, D], BF16)
    sgtm = overlay("C", 24 * 1024, "sgtm", [128, D], BF16)
    saB = overlay("C", 24 * 1024, "saB", [128, D], BF16)
    qtm = overlay("C", 28 * 1024, "qtm", [128, D], BF16)
    u_t = overlay("C", 28 * 1024, "u_t", [128, D], BF16)
    y2fm = overlay("C", 28 * 1024, "y2fm", [128, DC, 128], BF16)
    yhb = overlay("C", 32 * 1024, "yhb", [128, D], BF16)
    wstage = overlay("C", 36 * 1024, "wstage", [128, 2, D], BF16)
    gbB = overlay("C", 44 * 1024, "gbB", [128, D], BF16)

    pA = es.enter_context(nc.psum_tensor("pA", [128, D], F32))
    pB = es.enter_context(nc.psum_tensor("pB", [128, D], F32))
    pav = pA[:, :].bitcast(BF16)

    sems = {name: es.enter_context(nc.semaphore(name)) for name in SEMS}

    def rowap(r, off=0, n=None):
        n = D if n is None else n
        return rows6[0:1, r * D + off: r * D + off + n]

    def schedule(pp: P):
        t, v, s, g, y = pp.tensor, pp.vector, pp.scalar, pp.gpsimd, pp.sync
        inc, wait, at, dinc, waitm = pp.inc, pp.wait, pp.at, pp.dinc, pp.waitm
        marks = {}

        def ln_stats(src2d, ctx=0):
            nt = D // 512
            last = None
            for ii in range(nt):
                last = v.bn_stats(bnst[ctx][:, 6 * ii:6 * ii + 6],
                                  src2d[:, 512 * ii:512 * (ii + 1)])
            inc(last, "sV")
            wait("vector", "sV")
            mk = marks.get(("lnfree", ctx))
            if mk:
                wait("vector", "sS", mk)
            inc(v.bn_aggr(mv[ctx][:, :], bnst[ctx][:, 0:6 * nt]), "sV")

        def rstd_chain(ctx=0):
            wait("scalar", "sV")
            inc(s.activation(rstd[ctx][:, :], mv[ctx][:, 1:2], AF.Copy, bias=EPS,
                             scale=1.0), "sS")
            wait("scalar", "sS")
            inc(s.activation(rstd[ctx][:, :], rstd[ctx][:, :], AF.Ln), "sS")
            wait("scalar", "sS")
            inc(s.activation(rstd[ctx][:, :], rstd[ctx][:, :], AF.Exp, scale=-0.5), "sS")
            wait("scalar", "sS")
            inc(s.activation(nmr[ctx][:, :], mv[ctx][:, 0:1], AF.Identity,
                             scale=rstd[ctx][:, :]), "sS")
            wait("scalar", "sS")
            inc(s.mul(nmr[ctx][:, :], nmr[ctx][:, :], -1.0), "sS")
            wait("scalar", "sS")
            marks[("lnfree", ctx)] = at("sS")

        def normalize(dst, src, ctx=0):
            return inc(s.activation(dst, src, AF.Identity, bias=nmr[ctx][:, :],
                                    scale=rstd[ctx][:, :]), "sS")

        def dbgdma(out_ap, in_ap):
            mk = dinc(y.dma_start(out=out_ap, in_=in_ap), "dbx")
            waitm("vector", mk)
            waitm("scalar", mk)
            waitm("tensor", mk)
            waitm("gpsimd", mk)
            waitm("sync", mk)
            return mk

        def bcast_row(dst, src_row, addone=False):
            for bi in range(D // 512):
                bank = pA[:, (bi % 2) * 512:(bi % 2) * 512 + 512]
                if bi >= 2:
                    wait("tensor", "sV", marks["bc_ev"])
                mm = t.matmul(bank, onesr[0:1, 0:128], src_row[0:1, bi * 512:(bi + 1) * 512],
                              start=True, stop=True)
                inc(mm, "sT")
                wait("vector", "sT")
                if addone:
                    cp = v.tensor_scalar(dst[:, bi * 512:(bi + 1) * 512], bank, 1.0, None,
                                         ALU.add)
                else:
                    cp = v.tensor_copy(dst[:, bi * 512:(bi + 1) * 512], bank)
                marks["bc_ev"] = inc(cp, "sV")
            wait("tensor", "sV")

        def stage_weight(w_src, dst_slot, cast_dma, fold_gaB=False, tag="w",
                         first_wait_T=None, first_wait_dma=()):
            iss = "gpsimd" if cast_dma else "sync"
            se, so_ = ("wge", "wgo") if cast_dma else ("wse", "wso")
            for dc in range(DC):
                buf = dc % 2
                sem = se if buf == 0 else so_
                if dc >= 2:
                    wait(iss, "sT", marks[(tag + "t", dc - 2)])
                else:
                    if first_wait_T is not None:
                        wait(iss, "sT", first_wait_T)
                    for mk in first_wait_dma:
                        waitm(iss, mk)
                dma = (g.dma_start(out=wstage[:, buf, :], in_=w_src[dc * 128:(dc + 1) * 128, :])
                       if cast_dma else
                       y.dma_start(out=wstage[:, buf, :], in_=w_src[dc * 128:(dc + 1) * 128, :]))
                mk = dinc(dma, sem)
                waitm("tensor", mk)
                if dc >= 1:
                    wait("tensor", "sV", marks[(tag + "e", dc - 1)])
                last = None
                for k in range(DC):
                    last = t.transpose(pav[:, k * 128:(k + 1) * 128],
                                       wstage[:, buf, k * 128:(k + 1) * 128], identb[:, :])
                marks[(tag + "t", dc)] = inc(last, "sT")
                wait("vector", "sT")
                src3 = pav[:, 0:DC * 128].rearrange("p (k a) -> p k a", k=DC, a=128)
                if fold_gaB:
                    ga3 = bass.AP(gaB, dc * 128, [[D, 128], [0, DC], [1, 128]])
                    cp = v.tensor_tensor(dst_slot[:, :, dc * 128:(dc + 1) * 128], src3, ga3,
                                         ALU.mult)
                else:
                    cp = v.tensor_copy(dst_slot[:, :, dc * 128:(dc + 1) * 128], src3)
                marks[(tag + "e", dc)] = inc(cp, "sV")
            wait("tensor", "sV")
            return at("sV")

        for dst, src in ((identf[:, :], ident_in[:, :]), (mrealc[:, :], mreal_in[:, :]),
                         (mask16[:, :], mask16_in[:, :]), (lamrow[0:1, 0:H], beta_in[0:1, :]),
                         (lnwr[0:1, :], lnw_in[0:1, :]), (lnbr[0:1, :], lnb_in[0:1, :]),
                         (rows6[0:1, 0:D], cvec_in[0:1, :])):
            dinc(y.dma_start(out=dst, in_=src), "c0")
        marks["p0"] = ("c0", at("c0"))
        dinc(g.dma_start(out=identb[:, :], in_=ident_in[:, :]), "g0")
        dinc(g.dma_start(out=genD[:, :], in_=m_d_in[:, :]), "g0")
        dinc(g.dma_start(out=genP[:, :], in_=m_pre_in[:, :]), "g0")
        dinc(g.dma_start(out=genL[:, :], in_=m_l3_in[:, :]), "g0")
        marks["g0"] = ("g0", at("g0"))

        inc(v.memset(zcol[:, :], 0.0), "sV")
        inc(v.memset(onesr[:, :], 1.0), "sV")
        inc(v.memset(llrow[:, :], -1.0), "sV")
        marks["memsets"] = at("sV")

        waitm("scalar", marks["p0"])
        wait("scalar", "sV", marks["memsets"])
        inc(s.activation(lamrow[0:1, 0:H], lamrow[0:1, 0:H], AF.Sigmoid), "sS")
        wait("vector", "sS")
        inc(v.tensor_scalar(lamrow[0:1, 0:H], lamrow[0:1, 0:H], 1e-38, 1.0 - 1e-7,
                            ALU.max, ALU.min), "sV")
        if "lam" in dbg:
            wait("sync", "sV")
            dbgdma(dbg["lam"][0:1, :], lamrow[0:1, 0:H])
        wait("scalar", "sV")
        inc(s.activation(llrow[0:1, 0:H], lamrow[0:1, 0:H], AF.Ln), "sS")
        wait("tensor", "sS")
        waitm("tensor", marks["p0"])
        inc(t.transpose(pA[:, 0:1], llrow[0:1, :], identf[0:1, 0:1]), "sT")
        wait("vector", "sT")
        inc(v.tensor_copy(llcol[:, :], pA[:, 0:1]), "sV")
        wait("scalar", "sV")
        inc(s.activation(lam128[:, :], llcol[:, :], AF.Exp, scale=128.0), "sS")
        marks["lam128"] = at("sS")
        wait("tensor", "sV", marks["memsets"])
        inc(t.matmul(pA[:, 512:640], onesr[0:1, 0:128], llrow[0:1, 0:128],
                     start=True, stop=True), "sT")
        wait("vector", "sT")
        inc(v.tensor_copy(llB[:, :], pA[:, 512:640]), "sV")
        marks["llB"] = at("sV")

        waitm("vector", marks["g0"])
        wait("vector", "sV", marks["llB"])
        llDv = bass.AP(llB, 0, [[128, 128], [1, H], [0, 128]])
        inc(v.tensor_tensor(genD[:, :].rearrange("p (a b) -> p a b", a=H, b=128),
                            genD[:, :].rearrange("p (a b) -> p a b", a=H, b=128),
                            llDv, ALU.mult), "sV")
        wait("scalar", "sV")
        inc(s.activation(d_all[:, :], genD[:, :], AF.Exp), "sS")
        marks["d_all"] = at("sS")
        if "dall" in dbg:
            wait("sync", "sS")
            dbgdma(dbg["dall"][:, :], d_all[:, :])
        llPv = bass.AP(llB, 0, [[128, 128], [0, NPRE], [1, 128]])
        inc(v.tensor_tensor(genP[:, :].rearrange("p (a b) -> p a b", a=NPRE, b=128),
                            genP[:, :].rearrange("p (a b) -> p a b", a=NPRE, b=128),
                            llPv, ALU.mult), "sV")
        wait("scalar", "sV")
        inc(s.activation(lpre[:, :, :].rearrange("p a b -> p (a b)"), genP[:, :], AF.Exp), "sS")
        marks["lpre"] = at("sS")
        if "lpre" in dbg:
            wait("sync", "sS")
            dbgdma(dbg["lpre"][:, :], lpre[:, :, :].rearrange("p a b -> p (a b)"))
        wait("vector", "sS", marks["lam128"])
        inc(v.tensor_scalar(genL[:, :], genL[:, :], llcol[:, :], None, ALU.mult), "sV")
        wait("scalar", "sV")
        inc(s.activation(l3[:, :], genL[:, :], AF.Exp), "sS")
        marks["l3"] = at("sS")
        wait("gpsimd", "sS", marks["l3"])
        mk_msum = dinc(g.dma_start(out=genL[:, :], in_=m_sum_in[:, :]), "g0")
        waitm("vector", mk_msum)
        wait("vector", "sV", marks["llB"])
        inc(v.tensor_tensor(genL[:, :], genL[:, :], llB[:, :], ALU.mult), "sV")
        wait("scalar", "sV")
        inc(s.activation(lsum[:, :], genL[:, :], AF.Exp), "sS")
        marks["lsum"] = at("sS")

        waitm("scalar", marks["p0"])
        inc(s.activation(rows6[0:1, 0:D], rows6[0:1, 0:D], AF.Silu), "sS")
        wait("tensor", "sS")
        last = None
        for k in range(DC):
            last = t.transpose(pA[:, 1024 + k:1024 + k + 1],
                               rows6[0:1, k * 128:(k + 1) * 128], identf[0:1, 0:1])
        inc(last, "sT")
        wait("vector", "sT")
        inc(v.tensor_copy(scvf[:, :], pA[:, 1024:1024 + DC]), "sV")
        marks["scvf"] = at("sV")

        for it in range(max(NJ, NPRE)):
            if it < NJ:
                jb = it
                buf = jb % 2
                sem = "wme" if buf == 0 else "wmo"
                if jb == 0:
                    wait("gpsimd", "sS", marks["lpre"])
                if jb >= 2:
                    wait("gpsimd", "sT", marks[("wm_t", jb - 2)])
                src = bass.AP(wmod_in.tensor, jb * 512,
                              [[6 * D, 128], [128 * 6 * D, DC], [1, 512]])
                mk = dinc(g.dma_start(out=wmst[:, buf, :, :], in_=src), sem)
                waitm("tensor", mk)
                wait("tensor", "sV", marks["scvf"])
                if jb >= 1:
                    wait("tensor", "sV", marks[("wm_e", jb - 1)])
                last = None
                for k in range(DC):
                    last = t.matmul(pA[0:1, 1024:1536], scvf[:, k:k + 1], wmst[:, buf, k, :],
                                    start=(k == 0), stop=(k == DC - 1))
                marks[("wm_t", jb)] = inc(last, "sT")
                wait("vector", "sT")
                rr, cc = jb // NDB, (jb % NDB) * 512
                cp = v.tensor_copy(rows6[0:1, rr * D + cc:rr * D + cc + 512],
                                   pA[0:1, 1024:1536])
                marks[("wm_e", jb)] = inc(cp, "sV")
            if it < NPRE:
                pc = it
                buf = pc % 2
                sem = "xe" if buf == 0 else "xo"
                if pc >= 2:
                    wait("sync", "sS", marks[("pre_n", pc - 2)])
                mk = dinc(y.dma_start(out=xtf[:, buf, :],
                                      in_=xpre_in[pc * 128:(pc + 1) * 128, :]), sem)
                marks[("pre_ld", pc)] = mk
                waitm("vector", mk)
                ln_stats(xtf[:, buf, :])
                rstd_chain()
                marks[("pre_ch", pc)] = at("sS")
                if pc >= 1:
                    wait("scalar", "sT", marks[("pre_t", pc - 1)])
                marks[("pre_n", pc)] = normalize(xn0[:, :], xtf[:, buf, :])
                wait("tensor", "sS")
                wait("tensor", "sS", marks["lpre"])
                waitm("tensor", marks["p0"])
                for db in range(NDB):
                    t.matmul(pB[:, db * 512:(db + 1) * 512], lpre[:, pc, :],
                             xn0[:, db * 512:(db + 1) * 512],
                             start=(pc == 0), stop=(pc == NPRE - 1))
                last = t.matmul(pA[:, 1536:1537], lpre[:, pc, :], mrealc[:, pc:pc + 1],
                                start=(pc == 0), stop=(pc == NPRE - 1))
                marks[("pre_t", pc)] = inc(last, "sT")
        marks["mod_done"] = marks[("wm_e", NJ - 1)]
        marks["pre_done"] = marks[("pre_t", NPRE - 1)]
        if "modrow" in dbg:
            wait("sync", "sV", marks["mod_done"])
            dbgdma(dbg["modrow"][0:1, :], rows6[0:1, :])

        sgnr = rtmp[0:1, 0:D]
        absr = rtmp[0:1, D:2 * D]
        for r in (2, 5):
            wait("scalar", "sV", marks["mod_done"])
            wait("scalar", "sV")
            wait("scalar", "sS", marks["lsum"])
            inc(s.activation(sgnr, rowap(r), AF.Sign), "sS")
            wait("scalar", "sS")
            inc(s.activation(absr, rowap(r), AF.Abs), "sS")
            wait("vector", "sS")
            inc(v.tensor_scalar(absr, absr, 0.25, None, ALU.max), "sV")
            wait("vector", "sV")
            if "dbg_modrow" in marks:
                waitm("vector", marks["dbg_modrow"])
            inc(v.tensor_tensor(rowap(r), sgnr, absr, ALU.mult), "sV")
            marks[("gclamp", r)] = at("sV")

        wait("tensor", "sV")
        last = None
        for ci, r in enumerate((1, 0, 4, 3)):
            for k in range(DC):
                last = t.transpose(pA[:, 1024 + ci * DC + k:1024 + ci * DC + k + 1],
                                   rowap(r, k * 128, 128), identf[0:1, 0:1])
        inc(last, "sT")
        wait("vector", "sT")
        inc(v.tensor_copy(modc[:, :], pA[:, 1024:1024 + 4 * DC]), "sV")
        wait("vector", "sV")
        inc(v.tensor_scalar(modc[:, 0:DC], modc[:, 0:DC], 1.0, None, ALU.add), "sV")
        inc(v.tensor_scalar(modc[:, 2 * DC:3 * DC], modc[:, 2 * DC:3 * DC], 1.0, None,
                            ALU.add), "sV")
        marks["modc"] = at("sV")
        wait("tensor", "sV")

        marks["bc_ev"] = at("sV")
        bcast_row(lnwB[:, :], lnwr[0:1, :])
        bcast_row(lnbB[:, :], lnbr[0:1, :])
        wait("tensor", "sV", marks[("gclamp", 2)])
        bcast_row(gaB[:, :], rowap(2))
        wait("tensor", "sV", marks[("gclamp", 5)])
        bcast_row(gbB[:, :], rowap(5))
        bcast_row(caB[:, :], rowap(1), addone=True)
        bcast_row(saB[:, :], rowap(0))
        marks["btiles"] = at("sV")

        wait("vector", "sS", marks["lsum"])
        marks["wv_done"] = stage_weight(wv_in, wsA, True, tag="wv",
                                        first_wait_T=marks[("wm_t", NJ - 1)])
        if "wvT" in dbg:
            wait("sync", "sV", marks["wv_done"])
            dbgdma(dbg["wvT"][:, :, :], wsA[:, :, :])

        wait("vector", "sT", marks["pre_done"])
        wait("vector", "sV", marks["btiles"])
        inc(v.tensor_copy(sigc[:, :], pA[:, 1536:1537]), "sV")
        wait("vector", "sV")
        inc(v.tensor_tensor(pB[:, :], pB[:, :], caB[:, :], ALU.mult), "sV")
        wait("vector", "sV")
        inc(v.scalar_tensor_tensor(pB[:, :], saB[:, :], sigc[:, :], pB[:, :],
                                   ALU.mult, ALU.add), "sV")
        marks["ufold"] = at("sV")
        for k in range(DC):
            wait("vector", "sV", marks["ufold"])
            if k >= 1:
                wait("vector", "sT", marks[("ut_t", k - 1)])
            inc(v.tensor_copy(uft[:, :], pB[:, k * 128:(k + 1) * 128]), "sV")
            marks[("ut_c", k)] = at("sV")
            wait("tensor", "sV", marks[("ut_c", k)])
            if k >= 4:
                wait("tensor", "sV", marks[("ut_e", k - 4)])
            slot = k % 4
            marks[("ut_t", k)] = inc(
                t.transpose(pA[:, slot * 128:(slot + 1) * 128], uft[:, :], identf[:, :]), "sT")
            wait("vector", "sT", marks[("ut_t", k)])
            cp = v.tensor_copy(u_t[:, k * 128:(k + 1) * 128], pA[:, slot * 128:(slot + 1) * 128])
            marks[("ut_e", k)] = inc(cp, "sV")
        marks["u_t"] = at("sV")
        wait("tensor", "sV", marks["u_t"])
        last = None
        for db in range(NDB):
            for k in range(DC):
                last = t.matmul(pB[:, db * 512:(db + 1) * 512], u_t[:, k * 128:(k + 1) * 128],
                                wsA[:, k, db * 512:(db + 1) * 512],
                                start=(k == 0), stop=(k == DC - 1))
        inc(last, "sT")
        wait("vector", "sT")
        inc(v.tensor_tensor(carry[:, :], pB[:, :], mask16[:, :], ALU.mult), "sV")
        marks["carry"] = at("sV")
        if "carry0" in dbg:
            wait("sync", "sV", marks["carry"])
            dbgdma(dbg["carry0"][:, :], carry[:, :])

        wait("tensor", "sV", marks["carry"])
        marks["wq_done"] = stage_weight(wq_in, wsB, True, tag="wq",
                                        first_wait_T=marks[("wvt", DC - 1)])
        if "wqT" in dbg:
            wait("sync", "sV", marks["wq_done"])
            dbgdma(dbg["wqT"][:, :, :], wsB[:, :, :])

        def conv(dst, src, nrows, sem, step=256):
            for r0 in range(0, nrows, step):
                dinc(g.dma_start(out=dst[r0:r0 + step, :], in_=src[r0:r0 + step, :],
                                 max_dma_last_dim=8192), sem)
            return (sem, at(sem))
        conv(wg0f8, wg0_in, D, "cv1")
        marks["wgbf"] = conv(wgbf, wg_in, D, "cv1")
        conv(wg1f8, wg1_in, D, "cv2")
        marks["wobf"] = conv(wobf, wo_in, D, "cv2")
        marks["f8conv3"] = conv(wlf8, wl_in, HID, "cv3")

        xns = (xn0, xn1)

        def a_front(c):
            buf = c % 2
            sem = "xe" if buf == 0 else "xo"
            if c >= 2:
                wait("sync", "sS", marks[("a_n", c - 2)])
            else:
                wait("sync", "sS", marks[("pre_n", NPRE - 2 + c)])
            mk = dinc(y.dma_start(out=xtf[:, buf, :], in_=x_in[c * 128:(c + 1) * 128, :]), sem)
            waitm("vector", mk)
            ln_stats(xtf[:, buf, :])
            rstd_chain()
            if c >= 2:
                wait("scalar", "sT", marks[("a_ht", c - 2)])
            elif c == 0:
                wait("scalar", "sT", marks[("pre_t", NPRE - 1)])
            else:
                wait("scalar", "sT", marks[("wqt", DC - 1)])
            marks[("a_n", c)] = normalize(xns[c % 2][:, :], xtf[:, buf, :])

        def a_part2(c):
            rstd_chain(ctx=1)
            marks[("a_ch", c)] = at("sS")
            if c >= 1:
                waitm("scalar", marks[("a_sy", c - 1)])
            marks[("a_yh", c)] = normalize(yhb[:, :], qtm[:, :], ctx=1)
            if "yhat" in dbg:
                dbgdma(dbg["yhat"][c, :, :], yhb[:, :])
            wait("sync", "sS", marks[("a_yh", c)])
            marks[("a_sy", c)] = dinc(y.dma_start(out=yhat_s[c * 128:(c + 1) * 128, :],
                                                  in_=yhb[:, :]), "ys")

        a_front(0)
        for c in range(NC):
            xnc = xns[c % 2]
            wait("tensor", "sS", marks[("a_n", c)])
            if c >= 1:
                wait("tensor", "sV", marks[("a_y", c - 1)])
            last = None
            for k in range(DC):
                last = t.transpose(pav[:, k * 128:(k + 1) * 128], xnc[:, k * 128:(k + 1) * 128],
                                   identb[:, :])
            marks[("a_ht", c)] = inc(last, "sT")
            wait("vector", "sT")
            wait("vector", "sV", marks["modc"])
            last = None
            for k in range(DC):
                last = v.tensor_scalar(hfm[:, k, :], pav[:, k * 128:(k + 1) * 128],
                                       modc[:, k:k + 1], modc[:, DC + k:DC + k + 1],
                                       ALU.mult, ALU.add)
            marks[("a_h", c)] = inc(last, "sV")
            if "hfm" in dbg:
                dbgdma(dbg["hfm"][c, :, :], hfm[:, :, :].rearrange("p k a -> p (k a)"))
            if c >= 1:
                wait("vector", "sV", marks[("a_y", c - 1)])
                ln_stats(qtm[:, :], ctx=1)
                a_part2(c - 1)
            if c + 1 < NC:
                a_front(c + 1)
            wait("tensor", "sV", marks[("a_h", c)])
            wait("tensor", "sV", marks.get(("a_carry", c - 1), marks["carry"]))
            last = None
            for k in range(DC):
                for db in range(NDB):
                    last = t.matmul(pB[:, db * 512:(db + 1) * 512], hfm[:, k, :],
                                    wsA[:, k, db * 512:(db + 1) * 512],
                                    start=(k == 0), stop=(k == DC - 1))
            marks[("a_vmm", c)] = inc(last, "sT")
            wait("vector", "sT")
            inc(v.tensor_copy(vtm[:, :], pB[:, :]), "sV")
            marks[("a_v", c)] = at("sV")
            if "v" in dbg:
                dbgdma(dbg["v"][c, :, :], vtm[:, :])
            wait("tensor", "sV", marks["wq_done"])
            last = None
            for k in range(DC):
                for db in range(NDB):
                    last = t.matmul(pA[:, db * 512:(db + 1) * 512], hfm[:, k, :],
                                    wsB[:, k, db * 512:(db + 1) * 512],
                                    start=(k == 0), stop=(k == DC - 1))
            marks[("a_qmm", c)] = inc(last, "sT")
            wait("vector", "sT")
            if c >= 1:
                wait("vector", "sS", marks[("a_yh", c - 1)])
            inc(v.tensor_copy(qtm[:, :], pA[:, :]), "sV")
            marks[("a_q", c)] = at("sV")
            if "q" in dbg:
                dbgdma(dbg["q"][c, :, :], qtm[:, :])
            wait("tensor", "sV", marks[("a_v", c)])
            last = None
            for db in range(NDB):
                last = t.matmul(pB[:, db * 512:(db + 1) * 512], lsum[:, :],
                                vtm[:, db * 512:(db + 1) * 512], start=True, stop=True)
            inc(last, "sT")
            wait("vector", "sT")
            inc(v.tensor_tensor(pB[:, :], pB[:, :], mask16[:, :], ALU.mult), "sV")
            marks[("a_l16", c)] = at("sV")
            wait("tensor", "sV", marks[("a_q", c)])
            last = None
            for b in range(NDB):
                for hh in range(4):
                    h = b * 4 + hh
                    last = t.matmul(pA[:, h * 128:(h + 1) * 128],
                                    d_all[:, h * 128:(h + 1) * 128],
                                    vtm[:, h * 128:(h + 1) * 128],
                                    start=(hh == 0), stop=False)
                last = t.matmul(pA[:, b * 512:(b + 1) * 512], l3[:, :],
                                carry[:, b * 512:(b + 1) * 512], start=False, stop=True)
            marks[("a_st", c)] = inc(last, "sT")
            wait("vector", "sT", marks[("a_st", c)])
            inc(v.tensor_tensor(qtm[:, :], qtm[:, :], pA[:, :], ALU.mult), "sV")
            marks[("a_y", c)] = at("sV")
            if "y" in dbg:
                dbgdma(dbg["y"][c, :, :], qtm[:, :])
            wait("vector", "sV", marks[("a_l16", c)])
            inc(v.scalar_tensor_tensor(carry[:, :], carry[:, :], lam128[:, :], pB[:, :],
                                       ALU.mult, ALU.add), "sV")
            marks[("a_carry", c)] = at("sV")
            if "carry_all" in dbg:
                dbgdma(dbg["carry_all"][c, :, :], carry[:, :])
        wait("vector", "sV", marks[("a_y", NC - 1)])
        ln_stats(qtm[:, :], ctx=1)
        a_part2(NC - 1)
        marks["passA"] = at("sT")

        for k in range(DC):
            sem = "wse" if k % 2 == 0 else "wso"
            if k == 0:
                waitm("sync", marks["wgbf"])
                wait("sync", "sT", marks[("a_vmm", NC - 1)])
            marks[("wgld", k)] = dinc(
                y.dma_start(out=wsA[:, k, :], in_=wgbf[:, k * 128:(k + 1) * 128],
                            transpose=True), sem)
        for k in range(DC):
            sem = "woe" if k % 2 == 0 else "woo"
            if k == 0:
                waitm("sync", marks["wobf"])
                wait("sync", "sT", marks[("a_qmm", NC - 1)])
            marks[("wold", k)] = dinc(
                y.dma_start(out=wsB[:, k, :], in_=wobf[:, k * 128:(k + 1) * 128],
                            transpose=True), sem)
        waitm("vector", marks[("wold", DC - 1)])
        waitm("vector", marks[("wold", DC - 2)])
        wait("vector", "sV", marks["btiles"])
        last = None
        for k in range(DC):
            last = v.tensor_tensor(wsB[:, k, :], wsB[:, k, :], gaB[:, :], ALU.mult)
        marks["wo_done"] = inc(last, "sV")

        def b_front(c):
            buf = c % 2
            sem = "xe" if buf == 0 else "xo"
            if c >= 2:
                waitm("sync", marks[("b_x2", c - 2)])
            else:
                wait("sync", "sS", marks[("a_n", NC - 2 + c)])
            mkx = dinc(y.dma_start(out=xtf[:, buf, :], in_=x_in[c * 128:(c + 1) * 128, :]), sem)
            marks[("b_ldx", c)] = mkx
            waitm("vector", mkx)
            ln_stats(xtf[:, buf, :])
            rstd_chain()
            marks[("b_ch", c)] = at("sS")
            if c >= 2:
                wait("scalar", "sT", marks[("b_ht", c - 2)])
            else:
                wait("scalar", "sT", marks[("a_ht", NC - 2 + c)])
            marks[("b_n", c)] = normalize(xns[c % 2][:, :], xtf[:, buf, :])

        b_front(0)
        for c in range(NC):
            buf = c % 2
            if c >= 1:
                wait("sync", "sT", marks[("b_y2t", c - 1)])
            waitm("sync", marks[("a_sy", NC - 1)])
            waitm("sync", marks[("a_sy", c)])
            mky = dinc(y.dma_start(out=yhb[:, :], in_=yhat_s[c * 128:(c + 1) * 128, :]), "yl")
            marks[("b_ldy", c)] = mky
            wait("tensor", "sS", marks[("b_n", c)])
            if c >= 1:
                wait("tensor", "sV", marks[("b_y2fm", c - 1)])
            last = None
            for k in range(DC):
                last = t.transpose(pav[:, k * 128:(k + 1) * 128],
                                   xns[c % 2][:, k * 128:(k + 1) * 128], identb[:, :])
            marks[("b_ht", c)] = inc(last, "sT")
            wait("vector", "sT")
            last = None
            for k in range(DC):
                last = v.tensor_scalar(hfm[:, k, :], pav[:, k * 128:(k + 1) * 128],
                                       modc[:, k:k + 1], modc[:, DC + k:DC + k + 1],
                                       ALU.mult, ALU.add)
            marks[("b_h", c)] = inc(last, "sV")
            if c + 1 < NC:
                b_front(c + 1)
            wait("tensor", "sV", marks[("b_h", c)])
            waitm("tensor", marks[("wgld", DC - 1)])
            waitm("tensor", marks[("wgld", DC - 2)])
            if c >= 1:
                wait("tensor", "sV", marks[("b_r", c - 1)])
            last = None
            for k in range(DC):
                for db in range(NDB):
                    last = t.matmul(pB[:, db * 512:(db + 1) * 512], hfm[:, k, :],
                                    wsA[:, k, db * 512:(db + 1) * 512],
                                    start=(k == 0), stop=(k == DC - 1))
            marks[("b_g", c)] = inc(last, "sT")
            wait("scalar", "sT", marks[("b_g", c)])
            last = None
            for db in range(NDB):
                last = s.activation(sgtm[:, db * 512:(db + 1) * 512],
                                    pB[:, db * 512:(db + 1) * 512], AF.Silu)
            marks[("b_sg", c)] = inc(last, "sS")
            waitm("vector", marks[("b_ldy", c)])
            wait("vector", "sS", marks[("b_sg", c)])
            inc(v.tensor_tensor(yhb[:, :], yhb[:, :], lnwB[:, :], ALU.mult), "sV")
            wait("vector", "sV")
            inc(v.tensor_tensor(yhb[:, :], yhb[:, :], lnbB[:, :], ALU.add), "sV")
            wait("vector", "sV")
            inc(v.tensor_tensor(yhb[:, :], yhb[:, :], sgtm[:, :], ALU.mult), "sV")
            marks[("b_y2", c)] = at("sV")
            wait("tensor", "sV", marks[("b_y2", c)])
            wait("tensor", "sV", marks[("b_h", c)])
            last = None
            for k in range(DC):
                last = t.transpose(pav[:, k * 128:(k + 1) * 128], yhb[:, k * 128:(k + 1) * 128],
                                   identb[:, :])
            marks[("b_y2t", c)] = inc(last, "sT")
            wait("vector", "sT")
            src3 = pav[:, 0:DC * 128].rearrange("p (k a) -> p k a", k=DC, a=128)
            inc(v.tensor_copy(y2fm[:, :, :], src3), "sV")
            marks[("b_y2fm", c)] = at("sV")
            wait("tensor", "sV", marks[("b_y2fm", c)])
            wait("tensor", "sV", marks["wo_done"])
            last = None
            for k in range(DC):
                for db in range(NDB):
                    last = t.matmul(pB[:, db * 512:(db + 1) * 512], y2fm[:, k, :],
                                    wsB[:, k, db * 512:(db + 1) * 512],
                                    start=(k == 0), stop=(k == DC - 1))
            inc(last, "sT")
            wait("vector", "sT")
            inc(v.tensor_tensor(xtf[:, buf, :], pB[:, :], xtf[:, buf, :], ALU.add), "sV")
            marks[("b_r", c)] = at("sV")
            wait("sync", "sV", marks[("b_r", c)])
            sem2 = "x2se" if buf == 0 else "x2so"
            marks[("b_x2", c)] = dinc(y.dma_start(out=x2_s[c * 128:(c + 1) * 128, :],
                                                  in_=xtf[:, buf, :]), sem2)
            if "x2" in dbg:
                dbgdma(dbg["x2"][c * 128:(c + 1) * 128, :], xtf[:, buf, :])
        marks["passB"] = at("sV")

        for bi in range(NBLK):
            t0 = bi * FBLK

            def f_front(tt, bi=None, t0=None):
                bi_, t0_ = bi, t0
                buf = tt % 2
                gidx = bi_ * FTC + tt
                sem = "xe" if gidx % 2 == 0 else "xo"
                r0 = t0_ + tt * 128
                if tt >= 2:
                    wait("sync", "sS", marks[("f_n", bi_, tt - 2)])
                elif bi_ >= 1:
                    wait("sync", "sS", marks[("f_n", bi_ - 1, FTC - 2 + tt)])
                else:
                    waitm("sync", marks[("b_x2", NC - 2 + tt)])
                waitm("sync", marks[("b_x2", gidx)])
                mk = dinc(y.dma_start(out=xtf[:, buf, :], in_=x2_s[r0:r0 + 128, :]), sem)
                waitm("vector", mk)
                ln_stats(xtf[:, buf, :])
                rstd_chain()
                marks[("f_ch", bi_, tt)] = at("sS")
                if tt >= 2:
                    wait("scalar", "sT", marks[("f_ht", bi_, tt - 2)])
                elif bi_ >= 1:
                    wait("scalar", "sT", marks[("f_ht", bi_ - 1, FTC - 2 + tt)])
                else:
                    wait("scalar", "sT", marks[("b_ht", NC - 2 + tt)])
                marks[("f_n", bi_, tt)] = normalize(xns[tt % 2][:, :], xtf[:, buf, :])

            f_front(0, bi=bi, t0=t0)
            for tt in range(FTC):
                wait("tensor", "sS", marks[("f_n", bi, tt)])
                if tt >= 1:
                    wait("tensor", "sV", marks[("f_h", bi, tt - 1)])
                elif bi >= 1:
                    wait("tensor", "sV", marks[("f_res", bi - 1, NG - 1, FTC - 1)])
                else:
                    wait("tensor", "sV", marks[("b_y2fm", NC - 1)])
                last = None
                for k in range(DC):
                    last = t.transpose(pav[:, k * 128:(k + 1) * 128],
                                       xns[tt % 2][:, k * 128:(k + 1) * 128], identb[:, :])
                marks[("f_ht", bi, tt)] = inc(last, "sT")
                wait("vector", "sT")
                last = None
                for k in range(DC):
                    last = v.tensor_scalar(h2fm[:, k, tt * 128:(tt + 1) * 128],
                                           pav[:, k * 128:(k + 1) * 128],
                                           modc[:, 2 * DC + k:2 * DC + k + 1],
                                           modc[:, 3 * DC + k:3 * DC + k + 1],
                                           ALU.mult, ALU.add)
                marks[("f_h", bi, tt)] = inc(last, "sV")
                if tt + 1 < FTC:
                    f_front(tt + 1, bi=bi, t0=t0)
            if "h2fm" in dbg:
                wait("sync", "sV", marks[("f_h", bi, FTC - 1)])
                dbgdma(dbg["h2fm"][bi, :, :, :], h2fm[:, :, :])

            waitm("scalar", marks["wgbf"])
            waitm("scalar", marks["wobf"])
            def _wgdma(kk):
                bb = kk % 2
                sem_ = "fge" if bb == 0 else "fgo"
                s0 = bass.AP(wg0f8.tensor, kk * 128, [[HID, 128], [128 * HID, DC], [1, 128]])
                s1 = bass.AP(wg1f8.tensor, kk * 128, [[HID, 128], [128 * HID, DC], [1, 128]])
                dinc(s.dma_start(out=wg0b[:, bb, :, :], in_=s0), sem_)
                marks[("f_wg", bi, kk)] = dinc(s.dma_start(out=wg1b[:, bb, :, :], in_=s1), sem_)
            if bi == 0:
                _wgdma(0)
                _wgdma(1)
            else:
                wait("scalar", "sT", marks[("f_mm", bi - 1, KH - 2)])
                _wgdma(0)
                wait("scalar", "sT", marks[("f_mm", bi - 1, KH - 1)])
                _wgdma(1)
            for k in range(KH):
                buf = k % 2
                ps = pA if buf == 0 else pB
                waitm("tensor", marks[("f_wg", bi, k)])
                if k >= 2:
                    wait("tensor", "sV", marks[("f_act", bi, k - 2)])
                    wait("tensor", "sS", marks[("f_gel", bi, k - 2)])
                elif k == 0:
                    wait("tensor", "sV", marks[("f_h", bi, FTC - 1)])
                    if bi >= 1:
                        wait("tensor", "sV", marks[("f_act", bi - 1, KH - 1)])
                        wait("tensor", "sS", marks[("f_gel", bi - 1, KH - 1)])
                else:
                    if bi >= 1:
                        wait("tensor", "sV", marks[("f_res", bi - 1, NG - 1, FTC - 1)])
                    else:
                        wait("tensor", "sV", marks[("b_r", NC - 1)])
                last = None
                for j in range(DC // 2):
                    for hh in range(NH2):
                        last = t.matmul(ps[:, hh * 512:(hh + 1) * 512],
                                        wg0b[:, buf, 2 * j:2 * j + 2, :],
                                        h2fm[:, 2 * j:2 * j + 2, hh * 512:(hh + 1) * 512],
                                        start=(j == 0), stop=(j == DC // 2 - 1),
                                        perf_mode=PM.DoubleRow)
                        last = t.matmul(ps[:, FBLK + hh * 512:FBLK + (hh + 1) * 512],
                                        wg1b[:, buf, 2 * j:2 * j + 2, :],
                                        h2fm[:, 2 * j:2 * j + 2, hh * 512:(hh + 1) * 512],
                                        start=(j == 0), stop=(j == DC // 2 - 1),
                                        perf_mode=PM.DoubleRow)
                marks[("f_mm", bi, k)] = inc(last, "sT")
                wait("scalar", "sT", marks[("f_mm", bi, k)])
                if k + 2 < KH:
                    _wgdma(k + 2)
                if k >= 2:
                    wait("scalar", "sV", marks[("f_act", bi, k - 2)])
                elif bi >= 1:
                    wait("scalar", "sV", marks[("f_act", bi - 1, KH - 2 + k)])
                marks[("f_gel", bi, k)] = inc(
                    s.activation(gelb[:, buf, :], ps[:, 0:FBLK], AF.Gelu_apprx_tanh), "sS")
                wait("vector", "sS", marks[("f_gel", bi, k)])
                inc(v.tensor_tensor(act[:, k, :], gelb[:, buf, :], ps[:, FBLK:2 * FBLK],
                                    ALU.mult), "sV")
                marks[("f_act", bi, k)] = at("sV")
            if "act" in dbg:
                wait("sync", "sV", marks[("f_act", bi, KH - 1)])
                dbgdma(dbg["act"][bi, :, :, :], act[:, :, :])

            waitm("scalar", marks["f8conv3"])
            for gg in range(NG):
                buf = gg % 2
                ps = pA if buf == 0 else pB
                psv = ps[:, :].bitcast(BF16)
                sem = "fle" if buf == 0 else "flo"
                if gg >= 2:
                    wait("scalar", "sT", marks[("f_o", bi, gg - 2)])
                elif bi >= 1:
                    wait("scalar", "sT", marks[("f_o", bi - 1, NG - 2 + gg)])
                srcw = bass.AP(wlf8.tensor, gg * 256, [[D, 128], [128 * D, KH], [1, 256]])
                mkw = dinc(s.dma_start(out=wlb[:, buf, :, :], in_=srcw), sem)
                waitm("tensor", mkw)
                if gg >= 2:
                    wait("tensor", "sV", marks[("f_res", bi, gg - 2, FTC - 1)])
                elif gg == 0:
                    wait("tensor", "sV", marks[("f_act", bi, KH - 1)])
                    wait("tensor", "sS", marks[("f_gel", bi, KH - 1)])
                else:
                    wait("tensor", "sV", marks[("f_act", bi, KH - 2)])
                    wait("tensor", "sS", marks[("f_gel", bi, KH - 2)])
                last = None
                for j in range(KH // 2):
                    for do2 in range(2):
                        for hh in range(NH2):
                            last = t.matmul(
                                ps[:, (do2 * NH2 + hh) * 512:(do2 * NH2 + hh) * 512 + 512],
                                wlb[:, buf, 2 * j:2 * j + 2, do2 * 128:do2 * 128 + 128],
                                act[:, 2 * j:2 * j + 2, hh * 512:(hh + 1) * 512],
                                start=(j == 0), stop=(j == KH // 2 - 1),
                                perf_mode=PM.DoubleRow)
                marks[("f_o", bi, gg)] = inc(last, "sT")
                wait("vector", "sT", marks[("f_o", bi, gg)])
                if gg >= 1:
                    wait("vector", "sT", marks[("f_t", bi, gg - 1)])
                elif bi >= 1:
                    wait("vector", "sT", marks[("f_t", bi - 1, NG - 1)])
                last = None
                for do2 in range(2):
                    last = v.tensor_copy(oev[:, do2, :], ps[:, do2 * FBLK:do2 * FBLK + FBLK])
                marks[("f_oe", bi, gg)] = inc(last, "sV")
                wait("tensor", "sV", marks[("f_oe", bi, gg)])
                last = None
                for tt in range(FTC):
                    for do2 in range(2):
                        last = t.transpose(
                            psv[:, (tt * 2 + do2) * 128:(tt * 2 + do2 + 1) * 128],
                            oev[:, do2, tt * 128:(tt + 1) * 128], identb[:, :])
                marks[("f_t", bi, gg)] = inc(last, "sT")
                for tt in range(FTC):
                    pbuf = tt % 2
                    semp = "xpe" if pbuf == 0 else "xpo"
                    semo = "ose" if pbuf == 0 else "oso"
                    r0 = t0 + tt * 128
                    c0 = gg * 256
                    if tt >= 2:
                        wait("sync", "sV", marks[("f_res", bi, gg, tt - 2)])
                    mkp = dinc(y.dma_start(out=xpc[:, pbuf, :],
                                           in_=x2_s[r0:r0 + 128, c0:c0 + 256]), semp)
                    waitm("vector", mkp)
                    wait("vector", "sT", marks[("f_t", bi, gg)])
                    if ("f_prev_out", pbuf) in marks:
                        waitm("vector", marks[("f_prev_out", pbuf)])
                    inc(v.tensor_tensor(outp[:, pbuf, :], psv[:, tt * 256:(tt + 1) * 256],
                                        gbB[:, c0:c0 + 256], ALU.mult), "sV")
                    wait("vector", "sV")
                    inc(v.tensor_tensor(outp[:, pbuf, :], outp[:, pbuf, :], xpc[:, pbuf, :],
                                        ALU.add), "sV")
                    marks[("f_res", bi, gg, tt)] = at("sV")
                    wait("sync", "sV", marks[("f_res", bi, gg, tt)])
                    marks[("f_prev_out", pbuf)] = dinc(
                        y.dma_start(out=out_t[r0:r0 + 128, c0:c0 + 256], in_=outp[:, pbuf, :]),
                        semo)

        for semname in DMA_SEMS:
            wait("sync", semname)

    with nc.Block() as block:
        @block.sync
        def _(e):
            schedule(P(nc, "sync", sems, e))
        @block.gpsimd
        def _(e):
            schedule(P(nc, "gpsimd", sems, e))
        @block.tensor
        def _(e):
            schedule(P(nc, "tensor", sems, e))
        @block.vector
        def _(e):
            schedule(P(nc, "vector", sems, e))
        @block.scalar
        def _(e):
            schedule(P(nc, "scalar", sems, e))

    return nc


def make_inmaps(cfg, inputs):
    D, H, T, PRE = cfg["D"], cfg["H"], cfg["T"], cfg["PRE"]
    NPRE = PRE // 128
    x = np.ascontiguousarray(np.asarray(inputs["L0_attn_x"], dtype=np.float32))
    B, L, _ = x.shape
    nslice = L // T
    consts = make_consts(cfg)
    wgat = np.asarray(inputs["L0_ffn_w_gating"], np.float32)
    shared = dict(
        wq=np.ascontiguousarray(np.asarray(inputs["L0_attn_Wq"], np.float32)),
        wv=np.ascontiguousarray(np.asarray(inputs["L0_attn_Wv"], np.float32)),
        wg=np.ascontiguousarray(np.asarray(inputs["L0_attn_Wg"], np.float32)),
        wo=np.ascontiguousarray(np.asarray(inputs["L0_attn_Wo"], np.float32)),
        wg0=np.ascontiguousarray(wgat[0]),
        wg1=np.ascontiguousarray(wgat[1]),
        wl=np.ascontiguousarray(np.asarray(inputs["L0_ffn_w_linear"], np.float32)),
        wmod=np.ascontiguousarray(np.asarray(inputs["L0_wmod"], np.float32)),
        beta=np.asarray(inputs["L0_attn_beta"], np.float32).reshape(1, H),
        lnw=np.asarray(inputs["L0_attn_ln_w"], np.float32).reshape(1, D),
        lnb=np.asarray(inputs["L0_attn_ln_b"], np.float32).reshape(1, D),
        m_d=consts["m_d"], m_pre=consts["m_pre"], m_l3=consts["m_l3"],
        m_sum=consts["m_sum"], mask16=consts["mask16"], ident=consts["ident"],
    )
    cv = np.asarray(inputs["L0_c"], np.float32)
    in_maps = []
    for core in range(B * nslice):
        b, sl = core // nslice, core % nslice
        xpre = np.zeros((PRE, D), np.float32)
        npre_tok = min(PRE, sl * T)
        if npre_tok:
            xpre[PRE - npre_tok:, :] = x[b, sl * T - npre_tok:sl * T, :]
        mreal = np.zeros((128, NPRE), ml_dtypes.bfloat16)
        nreal = npre_tok // 128
        if nreal:
            mreal[:, NPRE - nreal:] = 1.0
        m = dict(shared)
        m["x"] = np.ascontiguousarray(x[b, sl * T:(sl + 1) * T, :])
        m["xpre"] = xpre
        m["mreal"] = mreal
        m["cvec"] = np.ascontiguousarray(cv[b:b + 1, :])
        in_maps.append(m)
    return in_maps


def kernel(**inputs):
    cfg = full_cfg()
    D, T = cfg["D"], cfg["T"]
    x = np.asarray(inputs["L0_attn_x"])
    B, L, _ = x.shape
    nslice = L // T
    in_maps = make_inmaps(cfg, inputs)
    nc = build_program(cfg)
    res = run_bass_kernel_spmd(nc, in_maps, core_ids=list(range(B * nslice)))
    out = np.empty((B, L, D), np.float32)
    for core in range(B * nslice):
        b, sl = core // nslice, core % nslice
        out[b, sl * T:(sl + 1) * T, :] = res.results[core]["out"]
    return out
